# version 20
# baseline (speedup 1.0000x reference)
from contextlib import ExitStack

import ml_dtypes
import numpy as np

import concourse.bass as bass
import concourse.mybir as mybir
from concourse.bass_utils import run_bass_kernel_spmd

F32 = mybir.dt.float32
BF16 = mybir.dt.bfloat16
AF = mybir.ActivationFunctionType
ALU = mybir.AluOpType
AX = mybir.AxisListType

MAGIC = float(1.5 * 2 ** 23)
E = 2048
KC = 16
TOK = 2048
NT = 16
HEADS = 16
INV_E = 1.0 / 2048.0
REG_CONSTS = [1e-5, 1e-8, -1.0, MAGIC, -0.5, INV_E, 1.0 / 127.0]

ENGINES = ("sync", "gpsimd", "tensor", "vector", "scalar")

SEL_NP = np.zeros((10, 4 * 128), dtype=ml_dtypes.bfloat16)
for _b in range(4):
    SEL_NP[2 * _b:2 * _b + 2, _b * 128:(_b + 1) * 128] = 1
SEL_NP[8:10, 3 * 128:4 * 128] = 1
NEGM_NP = np.zeros((10, 1), np.float32)
NEGM_NP[1::2, 0] = -1.0
IDENT_NP = np.eye(128, dtype=ml_dtypes.bfloat16)


class Plan:
    def __init__(self):
        self.steps = {e: [] for e in ENGINES}
        self.counts = {}
        self.last = {}
        self.eng_dma = {}

    def emit(self, eng, fn, waits=(), inc=None, amt=1):
        tok = None
        if inc is not None:
            self.counts[inc] = self.counts.get(inc, 0) + amt
            tok = (inc, self.counts[inc])
            self.last[eng] = tok
            if amt == 16:
                self.eng_dma.setdefault(eng, {})[inc] = self.counts[inc]
        self.steps[eng].append(
            (fn, tuple(w for w in waits if w is not None), inc, amt))
        return tok


def build_nc():
    nc = bass.Bass("TRN2", target_bir_lowering=False)

    x_d = nc.dram_tensor("x", [TOK, E], F32, kind="ExternalInput").ap()
    wqk_d = nc.dram_tensor("wqk", [E, E], F32, kind="ExternalInput").ap()
    wv_d = nc.dram_tensor("wv", [E, E], F32, kind="ExternalInput").ap()
    wu_d = nc.dram_tensor("wu", [E, E], F32, kind="ExternalInput").ap()
    wo_d = nc.dram_tensor("wo", [E, E], F32, kind="ExternalInput").ap()
    wf_d = nc.dram_tensor("wf", [E, E], F32, kind="ExternalInput").ap()
    bqk_d = nc.dram_tensor("bqk", [1, E], F32, kind="ExternalInput").ap()
    bv_d = nc.dram_tensor("bv", [1, E], F32, kind="ExternalInput").ap()
    bu_d = nc.dram_tensor("bu", [1, E], F32, kind="ExternalInput").ap()
    bo_d = nc.dram_tensor("bo", [1, E], F32, kind="ExternalInput").ap()
    fbias_d = nc.dram_tensor("fbias", [1, E], F32, kind="ExternalInput").ap()
    lng_d = nc.dram_tensor("lng", [1, E], F32, kind="ExternalInput").ap()
    lnb_d = nc.dram_tensor("lnb", [1, E], F32, kind="ExternalInput").ap()
    cdb_d = nc.dram_tensor("cdb", [1, 16], F32, kind="ExternalInput").ap()
    sel_d = nc.dram_tensor("sel", [10, 4 * 128], BF16, kind="ExternalInput").ap()
    negm_d = nc.dram_tensor("negm", [10, 1], F32, kind="ExternalInput").ap()
    ident_d = nc.dram_tensor("ident", [128, 128], BF16, kind="ExternalInput").ap()
    out_d = nc.dram_tensor("out", [TOK, E], F32, kind="ExternalOutput").ap()
    xnsp_d = nc.dram_tensor("xnsp", [128, NT * KC * 128], BF16).ap()
    qsp_d = nc.dram_tensor("qsp", [128, NT * E], BF16).ap()
    usp_d = nc.dram_tensor("usp", [128, NT * E], BF16).ap()
    g16_d = nc.dram_tensor("g16d", [1, E], F32).ap()
    cci_d = nc.dram_tensor("cci", [128, HEADS * 128], F32).ap()
    cco_d = nc.dram_tensor("cco", [128, HEADS * 128], F32).ap()

    def xnsp(i):
        return xnsp_d[:, i * KC * 128:(i + 1) * KC * 128]

    def qsp(i):
        return qsp_d[:, i * E:(i + 1) * E]

    def usp(i):
        return usp_d[:, i * E:(i + 1) * E]

    with ExitStack() as ctx:
        ec = ctx.enter_context
        WA = ec(nc.sbuf_tensor("WA", [128, KC, E], BF16))
        WB = ec(nc.sbuf_tensor("WB", [128, KC, E], BF16))
        SCR = ec(nc.sbuf_tensor("SCR", [128, 10, E], BF16))
        X2 = ec(nc.sbuf_tensor("X2", [128, 2, E], F32))
        EXTRA = ec(nc.sbuf_tensor("EXTRA", [128, 2, E], BF16))
        BIASBF = ec(nc.sbuf_tensor("BIASBF", [10, E], BF16))
        SEL = ec(nc.sbuf_tensor("SEL", [10, 4 * 128], BF16))
        NEGM = ec(nc.sbuf_tensor("NEGM", [10, 1], F32))
        KVBF = ec(nc.sbuf_tensor("KVBF", [128, HEADS * 128], BF16))
        IDENT = ec(nc.sbuf_tensor("IDENT", [128, 128], BF16))
        STATS = ec(nc.sbuf_tensor("STATS", [128, 576], F32))
        G16 = ec(nc.sbuf_tensor("G16", [128, KC], F32))
        XS16 = ec(nc.sbuf_tensor("XS16", [128, NT], F32))
        CONSTS = ec(nc.sbuf_tensor("CONSTS", [128, 16], F32))
        SQD = ec(nc.sbuf_tensor("SQD", [128, E], BF16))
        PMM = ec(nc.psum_tensor("PMM", [128, 2, 512], F32))
        PTR = ec(nc.psum_tensor("PTR", [128, 2, 1024], BF16))
        PKV = ec(nc.psum_tensor("PKV", [128, 4, 512], F32))

        sems = {}
        for s in ("pe", "dv", "ac", "cc", "init", "rep", "g16s",
                  "wta", "wtA1", "wtA2", "wtA3", "wtb", "wtc", "ox", "ccd", "ccd2", "wf0", "wf1", "wf2", "wf3",
                  "bsy", "bgd", "bac",
                  "ds0", "ds1", "ds2", "ds3", "ds4", "ds5",
                  "sp0", "sp1", "sp2", "sp3",
                  "oa0", "oa1", "ob0", "ob1"):
            sems[s] = ec(nc.semaphore(s))

        for i, v in enumerate(REG_CONSTS):
            nc.const_aps.register(F32, v, CONSTS[:, i:i + 1])

        P = Plan()

        def dma_sy(sem, out, in_, waits=()):
            return P.emit("sync", lambda e, o=out, i=in_: e.dma_start(out=o, in_=i),
                          waits=waits, inc=sem, amt=16)

        def dma_gd(sem, out, in_, waits=(), accum=False):
            if accum:
                return P.emit(
                    "gpsimd",
                    lambda e, o=out, i=in_: e.dma_start(out=o, in_=i,
                                                        accum_op=ALU.add),
                    waits=waits, inc=sem, amt=16)
            return P.emit("gpsimd",
                          lambda e, o=out, i=in_: e.dma_start(out=o, in_=i),
                          waits=waits, inc=sem, amt=16)

        def pe(fn, waits=()):
            return P.emit("tensor", fn, waits=waits, inc="pe")

        def dv(fn, waits=()):
            return P.emit("vector", fn, waits=waits, inc="dv")

        def ac(fn, waits=()):
            return P.emit("scalar", fn, waits=waits, inc="ac")

        def barrier():
            toks = []
            for eng, bsem in (("sync", "bsy"), ("gpsimd", "bgd"),
                              ("scalar", "bac")):
                dws = tuple((s, v) for s, v in
                            P.eng_dma.get(eng, {}).items())
                toks.append(P.emit(eng, lambda e: e.nop(),
                                   waits=dws, inc=bsem))
            for eng in ("tensor", "vector"):
                if P.last.get(eng):
                    toks.append(P.last[eng])
            if P.last.get("scalar") and P.last["scalar"][0] == "ac":
                toks.append(P.last["scalar"])
            return tuple(toks)

        def dma_ac(sem, out, in_, waits=()):
            return P.emit("scalar", lambda e, o=out, i=in_: e.dma_start(out=o, in_=i),
                          waits=waits, inc=sem, amt=16)

        def scr_bf(i):
            return SCR[:, i, :]

        def scr_f32(i):
            return SCR[:, i:i + 2, :].bitcast(F32).rearrange("p a b -> p (a b)")

        def wa_bf(i):
            return WA[:, i, :]

        def wb_bf(i):
            return WB[:, i, :]

        def wa_fm(i):
            return WA[:, i, :].rearrange("p (c t) -> p c t", c=KC)

        def wb_fm(i):
            return WB[:, i, :].rearrange("p (c t) -> p c t", c=KC)

        def wa_f32(i):
            return WA[:, i:i + 2, :].bitcast(F32).rearrange("p a b -> p (a b)")

        def wb_f32(i):
            return WB[:, i:i + 2, :].bitcast(F32).rearrange("p a b -> p (a b)")

        def ex_bf(i):
            return EXTRA[:, i, :]

        def ex_f32():
            return EXTRA[:, 0:2, :].bitcast(F32).rearrange("p a b -> p (a b)")

        def psum_dummy(i):
            return SQD[:, :]

        def load_weight_chunks(sem, wdst, wsrc, c0, c1, waits=()):
            src = wsrc.rearrange("(c p) m -> p c m", p=128)[:, c0:c1, :]
            return dma_gd(sem, wdst[:, c0:c1, :], src, waits=waits)

        def load_weight_chunk1(sem, wdst, wsrc, c, waits=()):
            src = wsrc.rearrange("(c p) m -> p c m", p=128)[:, c:c + 1, :]
            return dma_gd(sem, wdst[:, c:c + 1, :], src, waits=waits)

        st_col = [0]

        def stat(n=1):
            c = st_col[0]
            st_col[0] += n
            assert st_col[0] <= 576
            return STATS[:, c:c + n]

        def transpose_tile(src_bf, dst_fm, src_tok, dst_free_tok, free,
                           evac="dv"):
            last = None
            emitf = dv if evac == "dv" else ac
            for h in range(2):
                t_tr = pe(lambda e, h=h, s=src_bf: [
                    e.transpose(PTR[:, h, c * 128:(c + 1) * 128],
                                s[:, (h * 8 + c) * 128:(h * 8 + c + 1) * 128],
                                IDENT[:])
                    for c in range(8)][-1],
                    waits=(src_tok, free.get(("ptr", h))))
                if evac == "dv":
                    t_ev = dv(lambda e, h=h, d=dst_fm: e.tensor_copy(
                        d[:, h * 8:(h + 1) * 8, :].rearrange("p c t -> p (c t)"),
                        PTR[:, h, :]),
                        waits=(t_tr, dst_free_tok))
                else:
                    t_ev = ac(lambda e, h=h, d=dst_fm: e.activation(
                        d[:, h * 8:(h + 1) * 8, :].rearrange("p c t -> p (c t)"),
                        PTR[:, h, :], AF.Copy),
                        waits=(t_tr, dst_free_tok))
                free[("ptr", h)] = t_ev
                last = t_ev
            return last

        def ln_chain(xb, sq_dst, t_x, sq_free, extra=()):
            s1 = stat(); ss = stat(); a = stat(); vp = stat()
            lnv = stat(); rstd = stat(); nb = stat()
            t_s1 = dv(lambda e, xb=xb, s1=s1: e.reduce_sum(s1, xb, axis=AX.X),
                      waits=(t_x,))
            t_ss = ac(lambda e, xb=xb, ss=ss, d=sq_dst: e.activation(
                d, xb, AF.Square, accum_out=ss), waits=(t_x, sq_free))
            t_a = dv(lambda e, s1=s1, a=a: e.scalar_tensor_tensor(
                out=a, in0=s1, scalar=-INV_E, in1=s1,
                op0=ALU.mult, op1=ALU.mult), waits=(t_s1,))
            t_vp = dv(lambda e, a=a, ss=ss, vp=vp: e.tensor_add(vp, a, ss),
                      waits=(t_a, t_ss))
            t_lnv = ac(lambda e, vp=vp, lnv=lnv: e.activation(
                lnv, vp, AF.Ln, bias=1e-5, scale=INV_E),
                waits=(t_vp,) + tuple(extra))
            t_rstd = ac(lambda e, lnv=lnv, rstd=rstd: e.activation(
                rstd, lnv, AF.Exp, scale=-0.5), waits=(t_lnv,))
            t_nb = dv(lambda e, s1=s1, rstd=rstd, nb=nb: e.scalar_tensor_tensor(
                out=nb, in0=s1, scalar=-INV_E, in1=rstd,
                op0=ALU.mult, op1=ALU.mult), waits=(t_rstd,))
            return rstd, nb, t_nb, t_ss

        free = {}

        tok_consts = dma_sy("init", CONSTS[:, :],
                            bass.AP(cdb_d.tensor, 0, [[0, 128], [1, 16]]))
        tok_ident = dma_sy("init", IDENT[:, :], ident_d)
        tok_sel = dma_sy("init", SEL[:, :], sel_d)
        tok_negm = dma_sy("init", NEGM[:, :], negm_d)

        bstage = SCR[0:10, 8:10, :].bitcast(F32).rearrange("p a b -> p (a b)")
        for b, bd in enumerate((bqk_d, bv_d, bu_d, bo_d, fbias_d)):
            dma_sy("init",
                   SCR[2 * b:2 * b + 2, 8:10, :].bitcast(F32)
                   .rearrange("p a b -> p (a b)"),
                   bass.AP(bd.tensor, 0, [[0, 2], [1, E]]))
        init_all = ("init", P.counts["init"])
        tok_hi = dv(lambda e: e.tensor_copy(BIASBF[:, :], bstage),
                    waits=(init_all,))
        tok_lo = dv(lambda e: e.scalar_tensor_tensor(
            out=BIASBF[:, :], in0=BIASBF[:, :], scalar=NEGM[:, :], in1=bstage,
            op0=ALU.mult, op1=ALU.add),
            waits=(tok_hi,))
        tok_wa_pairs = [load_weight_chunks("wta", WA, wqk_d, 0, 2),
                        load_weight_chunks("wtA1", WA, wqk_d, 2, 4),
                        load_weight_chunks("wtA2", WA, wqk_d, 4, 6),
                        load_weight_chunks("wtA3", WA, wqk_d, 6, 8)]
        wa_cast_toks = []
        for _c in range(8, KC):
            stg8 = scr_f32(4 if _c % 2 == 0 else 6)
            t_wl = dma_sy("ds%d" % (4 + _c % 2), stg8,
                          wqk_d[_c * 128:(_c + 1) * 128, :],
                          waits=(wa_cast_toks[-2] if len(wa_cast_toks) >= 2
                                 else None,))
            t_wc = dv(lambda e, s=stg8, c=_c: e.tensor_copy(WA[:, c, :], s),
                      waits=(t_wl,))
            wa_cast_toks.append(t_wc)
        tok_wa = wa_cast_toks[-1]
        INIT = (init_all, tok_lo)

        PTRF = [PTR[:, h, :].bitcast(F32) for h in range(2)]
        BANKS_A = [PMM[:, 0, :], PMM[:, 1, :], PKV[:, 0, :], PKV[:, 1, :]]
        BANKS_B = [PMM[:, 0, :], PMM[:, 1, :], PTRF[0], PTRF[1]]
        PHYS_A = (0, 1, 4, 5)
        PHYS_B = (0, 1, 2, 3)

        def proj_mm(bank_ap, fm, W, j, brow, ks, waits):
            def fn(e, bank_ap=bank_ap, fm=fm, W=W, j=j, brow=brow, ks=tuple(ks)):
                for idx, c in enumerate(ks):
                    e.matmul(bank_ap, fm[:, c, :],
                             W[:, c, j * 512:(j + 1) * 512],
                             start=(idx == 0), stop=False)
                return e.matmul(bank_ap,
                                SEL[:, brow * 128:(brow + 1) * 128],
                                BIASBF[:, j * 512:(j + 1) * 512],
                                start=False, stop=True)
            return pe(fn, waits=waits)

        free[("ptr", 0)] = init_all
        free[("ptr", 1)] = init_all
        I_XNFM, I_QTM = (1, 2), (3, 4)
        elu_scr = [scr_f32(0), scr_f32(2)]
        outx_toks = []
        wv_pre = [0, 5] + list(range(6, KC))
        korder2 = wv_pre[2:] + wv_pre[:2] + [1, 2, 3, 4]
        korder2 = list(korder2)

        stA = {}
        stB = {}

        def I_A(i):
            xb = X2[:, i % 2, :]
            t_x = dma_ac("ds%d" % (i % 2), xb, x_d[i * 128:(i + 1) * 128, :],
                         waits=(free.get(("x", i % 2)),))
            rstd, nb, t_nb, t_ss = ln_chain(xb, psum_dummy(0), t_x,
                                            free.get(("sqd", 0)),
                                            extra=((init_all,) if i < 2 else ()))
            free[("sqd", 0)] = t_ss
            t_xnbf = dv(lambda e, xb=xb, rstd=rstd, nb=nb, i=i: e.tensor_scalar(
                ex_bf(i % 2), xb, rstd, nb, ALU.mult, ALU.add),
                waits=(t_nb, free.get(("xnbf", i % 2))))
            free[("x", i % 2)] = t_xnbf
            stA[i] = t_xnbf

        def I_B(i):
            t_xnbf = stA.pop(i)
            fmslot = I_XNFM[i % 2]
            t_xnfm = transpose_tile(ex_bf(i % 2), wb_fm(fmslot), t_xnbf,
                                    free.get(("xnfm", i % 2)), free)
            free[("xnbf", i % 2)] = t_xnfm
            t_sp_xn = dma_gd("sp%d" % (i % 2), xnsp(i), wb_bf(fmslot),
                             waits=(t_xnfm,))
            stB[i] = (t_xnfm, t_sp_xn)

        def I_C(i):
            t_xnfm, t_sp_xn = stB.pop(i)
            fmslot = I_XNFM[i % 2]
            qslot = I_QTM[i % 2]
            t_q_last = None
            for j in range(4):
                bank_ap = BANKS_A[j]
                pb = PHYS_A[j]
                if i == 0:
                    for ci, c in enumerate(range(KC)):
                        wv = (tok_wa_pairs[c // 2] if c < 8
                              else wa_cast_toks[c - 8])
                        pe(lambda e, bank_ap=bank_ap, fmslot=fmslot, c=c, ci=ci, j=j:
                           e.matmul(bank_ap, wb_fm(fmslot)[:, c, :],
                                    WA[:, c, j * 512:(j + 1) * 512],
                                    start=(ci == 0), stop=False),
                           waits=(t_xnfm, free.get(("bank", pb)), wv))
                    t_mm = pe(lambda e, bank_ap=bank_ap, j=j: e.matmul(
                        bank_ap, SEL[:, 0:128],
                        BIASBF[:, j * 512:(j + 1) * 512],
                        start=False, stop=True),
                        waits=((tok_lo,) if j == 0 else ()))
                else:
                    t_mm = proj_mm(bank_ap, wb_fm(fmslot), WA, j, 0, range(KC),
                                   waits=(t_xnfm, free.get(("bank", pb)),
                                          (tok_wa if i == 1 else None))
                                   + (tuple(tok_wa_pairs) if i == 1 else ()))
                eslot = j % 2
                ebuf = elu_scr[eslot][:, 0:512]
                rbuf = elu_scr[eslot][:, 512:1024]
                sbuf = elu_scr[eslot][:, 1024:1536]
                t_e = ac(lambda e, bank_ap=bank_ap, eb=ebuf: e.activation(
                    eb, bank_ap, AF.Exp),
                    waits=(t_mm, free.get(("elu", eslot))))
                t_r = ac(lambda e, bank_ap=bank_ap, rb=rbuf: e.activation(
                    rb, bank_ap, AF.Relu), waits=(t_mm,))
                t_s = dv(lambda e, eb=ebuf, rb=rbuf, sb=sbuf: e.scalar_tensor_tensor(
                    out=sb, in0=eb, scalar=1.0, in1=rb, op0=ALU.min, op1=ALU.add),
                    waits=(t_e, t_r, free.get(("elus", eslot))))
                free[("bank", pb)] = t_r
                t_q = ac(lambda e, sb=sbuf, j=j, qslot=qslot: e.activation(
                    wb_bf(qslot)[:, j * 512:(j + 1) * 512], sb, AF.Copy, bias=-1.0),
                    waits=(t_s, free.get(("qtm", i % 2))))
                free[("elu", eslot)] = t_q
                free[("elus", eslot)] = t_q
                t_q_last = t_q
            t_sp_q = dma_gd("sp%d" % (i % 2), qsp(i), wb_bf(qslot),
                            waits=(t_q_last,))
            free[("xnfm", i % 2)] = t_sp_q
            free[("qtm", i % 2)] = t_sp_q
            if i < len(wv_pre):
                tok_wv_hi = load_weight_chunk1("wtb", WB, wv_d, wv_pre[i])
            if i == 14:
                for qn in (0, 1):
                    outx_toks.append(dma_gd(
                        "ox", out_d[qn * 512:(qn + 1) * 512, :],
                        x_d[qn * 512:(qn + 1) * 512, :]))

        for k in range(NT + 2):
            if 1 <= k <= NT:
                I_B(k - 1)
            if k < NT:
                I_A(k)
            if k >= 2:
                I_C(k - 2)

        BAR1 = barrier()

        tok_wv_lo = load_weight_chunks("wtc", WB, wv_d, 1, 5, waits=BAR1)
        pe_fence1 = ("pe", P.counts.get("pe", 0))

        II_XNFM, II_QIN, II_VTM = (0, 1), (2, 3), 4
        kv_last = None
        tok_wu_last = None
        for i in range(NT):
            fm = SCR[:, II_XNFM[i % 2], :].rearrange("p (c t) -> p c t", c=KC)
            spall = ((("sp0", P.counts["sp0"]), ("sp1", P.counts["sp1"]))
                     if i < 2 else ())
            t_xn = dma_sy("ds%d" % (2 + i % 2), SCR[:, II_XNFM[i % 2], :], xnsp(i),
                          waits=(free.get(("IIxn", i % 2)),) + spall
                          + (BAR1 if i < 2 else ()))
            t_qi = dma_sy("ds%d" % (4 + i % 2), scr_bf(II_QIN[i % 2]), qsp(i),
                          waits=(free.get(("IIq", i % 2)),) + spall
                          + (BAR1 if i < 2 else ()))
            t_v_last = None
            for j in range(4):
                bank_ap = BANKS_B[j]
                pb = PHYS_B[j]
                if i == 0:
                    for ci, c in enumerate(korder2):
                        wv = (tok_wv_lo if c in (1, 2, 3, 4)
                              else ("wtb", 16 * 12))
                        pe(lambda e, bank_ap=bank_ap, fm=fm, c=c, ci=ci, j=j:
                           e.matmul(bank_ap, fm[:, c, :],
                                    WB[:, c, j * 512:(j + 1) * 512],
                                    start=(ci == 0), stop=False),
                           waits=(t_xn, free.get(("bank", pb)), wv))
                    t_mm = pe(lambda e, bank_ap=bank_ap, j=j: e.matmul(
                        bank_ap, SEL[:, 128:256],
                        BIASBF[:, j * 512:(j + 1) * 512],
                        start=False, stop=True), waits=())
                else:
                    t_mm = proj_mm(bank_ap, fm, WB, j, 1, korder2,
                                   waits=(t_xn, free.get(("bank", pb))))
                t_v = ac(lambda e, j=j, bank_ap=bank_ap: e.activation(
                    scr_bf(II_VTM)[:, j * 512:(j + 1) * 512], bank_ap,
                    AF.Silu), waits=(t_mm, free.get(("vtm", 0))))
                free[("bank", pb)] = t_v
                t_v_last = t_v

            def kv_fn(e, i=i, qs=II_QIN[i % 2]):
                last = None
                for h in range(HEADS):
                    last = e.matmul(
                        PKV[:, h // 4, (h % 4) * 128:(h % 4 + 1) * 128],
                        scr_bf(qs)[:, h * 128:(h + 1) * 128],
                        scr_bf(II_VTM)[:, h * 128:(h + 1) * 128],
                        start=(i == 0 and h % 4 == 0),
                        stop=(i == NT - 1 and h % 4 == 3))
                return last
            t_kv = pe(kv_fn, waits=(t_qi, t_v_last))
            kv_last = t_kv
            free[("vtm", 0)] = t_kv
            free[("IIxn", i % 2)] = t_kv
            free[("IIq", i % 2)] = t_kv
            if i < 8:
                for c in (korder2[2 * i], korder2[2 * i + 1]):
                    tok_wu_last = load_weight_chunk1(
                        "wta", WA, wu_d, c,
                        waits=((pe_fence1,) if i == 0 else ()))

        wf_toks = {}
        wf_toks[0] = dma_gd("wf0", X2[:, 1, :], wf_d[0:128, :],
                            waits=(free.get(("x", 1)),))
        wf_toks[1] = dma_gd("wf1", scr_f32(8), wf_d[128:256, :],
                            waits=(tok_lo,))
        wf_toks[2] = dma_gd("wf2", scr_f32(4), wf_d[256:384, :],
                            waits=(kv_last,))
        wf_toks[3] = dma_gd("wf3", scr_f32(6), wf_d[384:512, :],
                            waits=(tok_wa,))
        kvsb = X2[:, 0, :]
        t_kvev = dv(lambda e: [e.tensor_copy(kvsb[:, b * 512:(b + 1) * 512],
                                             PKV[:, b, :]) for b in range(4)][-1],
                    waits=(kv_last,))

        BAR2 = barrier()

        III_XNFM = (0, 1)
        utm_bufs = [scr_bf(2), ex_bf(1)]
        wtnk = [scr_bf(3), ex_bf(0)]
        stage_f = [X2[:, 1, :], scr_f32(8), scr_f32(4), scr_f32(6)]
        MAGICR = CONSTS[:, 3:4]
        NEG1R = CONSTS[:, 2:3]

        t_g16d = None
        t_cci = None
        t_kvbf = None
        stP1 = {}
        stP2 = {}

        def III_wf(c):
            wf_toks[c] = dma_gd("wf%d" % (c % 4), stage_f[c % 4],
                                wf_d[c * 128:(c + 1) * 128, :],
                                waits=(free.get(("stage", c % 4)),))

        def III_P1(c):
            stg = stage_f[c % 4]
            t_wf = wf_toks.pop(c)
            red = stat(); ge = stat(); ginv = stat()
            t_red = dv(lambda e, stg=stg, red=red: e.reduce_sum(
                red, stg, axis=AX.X, apply_absolute_value=True), waits=(t_wf,))
            t_g = dv(lambda e, red=red, c=c: e.tensor_scalar(
                G16[:, c:c + 1], red, INV_E, None, ALU.mult), waits=(t_red,))
            t_ge = dv(lambda e, ge=ge, c=c: e.tensor_scalar(
                ge, G16[:, c:c + 1], 1e-8, None, ALU.add), waits=(t_g,))
            t_gi = dv(lambda e, ge=ge, ginv=ginv: e.reciprocal(ginv, ge),
                      waits=(t_ge,))
            stP1[c] = (t_gi, ginv, t_g)

        def III_P2(c):
            t_gi, ginv, t_g = stP1.pop(c)
            stg = stage_f[c % 4]
            t_d1 = dv(lambda e, stg=stg, ginv=ginv: e.tensor_scalar(
                stg, stg, ginv, MAGICR, ALU.mult, ALU.add), waits=(t_gi,))
            t_c1 = dv(lambda e, stg=stg: e.tensor_scalar(
                stg, stg, MAGICR, NEG1R, ALU.subtract, ALU.max),
                waits=(t_d1,))
            t_c2 = dv(lambda e, stg=stg, c=c: e.tensor_scalar(
                wtnk[c % 2], stg, 1.0, None, ALU.min),
                waits=(t_c1, free.get(("wtnk", c % 2))))
            free[("stage", c % 4)] = t_c2
            t_g16c = dma_gd("g16s", g16_d[0:1, c * 128:(c + 1) * 128],
                            G16[:, c:c + 1], waits=(t_g,))
            stP2[c] = (t_c2, t_g16c)

        t_g16_last = [None]

        def III_P3(c):
            t_c2, t_g16c = stP2.pop(c)
            t_g16_last[0] = t_g16c
            last = None
            for h in range(2):
                t_tr = pe(lambda e, h=h, c=c: [
                    e.transpose(PTR[:, h, k * 128:(k + 1) * 128],
                                wtnk[c % 2][:, (h * 8 + k) * 128:(h * 8 + k + 1) * 128],
                                IDENT[:])
                    for k in range(8)][-1],
                    waits=(t_c2, free.get(("ptr", h))))
                t_ev = dv(lambda e, h=h, c=c: e.tensor_copy(
                    WB[:, h * 8:(h + 1) * 8, c * 128:(c + 1) * 128],
                    PTR[:, h, :].rearrange("p (a b) -> p a b", a=8)),
                    waits=(t_tr,) + (BAR2 if c <= 1 else ()))
                free[("ptr", h)] = t_ev
                last = t_ev
            free[("wtnk", c % 2)] = last
            return t_g16c

        III_P1(0)
        III_P1(1)

        for i in range(NT):
            fm = SCR[:, III_XNFM[i % 2], :].rearrange("p (c t) -> p c t", c=KC)
            t_xn = dma_sy("ds%d" % (2 + i % 2), SCR[:, III_XNFM[i % 2], :], xnsp(i),
                          waits=(free.get(("IIIxn", i % 2)),) + (BAR2 if i < 2 else ()))
            t_u_last = None
            for j in range(4):
                bank_ap = BANKS_A[j]
                pb = PHYS_A[j]
                t_mm = proj_mm(bank_ap, fm, WA, j, 2, korder2,
                               waits=(t_xn, free.get(("bank", pb)),
                                      (tok_wu_last if i == 0 and j == 0 else None)))
                t_u = ac(lambda e, j=j, bank_ap=bank_ap, i=i: e.activation(
                    utm_bufs[i % 2][:, j * 512:(j + 1) * 512], bank_ap,
                    AF.Silu), waits=(t_mm, free.get(("utm", i % 2))))
                free[("bank", pb)] = t_u
                t_u_last = t_u
            t_sp_u = dma_ac("sp%d" % (2 + i % 2), usp(i), utm_bufs[i % 2],
                            waits=(t_u_last,))
            free[("utm", i % 2)] = t_sp_u
            free[("IIIxn", i % 2)] = t_sp_u
            if i in (6, 7):
                qn = 2 + (i - 6)
                outx_toks.append(dma_gd(
                    "ox", out_d[qn * 512:(qn + 1) * 512, :],
                    x_d[qn * 512:(qn + 1) * 512, :]))
            if i == 9:
                t_cci = dma_gd("ccd", cci_d, kvsb, waits=(t_kvev,))
                P.emit("gpsimd", lambda e: e.collective_compute(
                    "AllReduce", ALU.add,
                    replica_groups=[[0, 1, 2, 3], [4, 5, 6, 7]],
                    ins=[cci_d.opt()], outs=[cco_d.opt()]),
                    waits=(t_cci,), inc="cc")
            if i == 10:
                t_kvbf = dma_gd("ccd2", KVBF[:, :], cco_d, waits=(("cc", 1),))

            if 1 <= i <= 8:
                III_P3(2 * (i - 1))
                III_P3(2 * i - 1)
            if i <= 7:
                III_P2(2 * i)
                III_P2(2 * i + 1)
            if i <= 6:
                III_P1(2 * i + 2)
                III_P1(2 * i + 3)
            if i <= 5:
                III_wf(2 * i + 4)
                III_wf(2 * i + 5)
            if i == 8:
                t_g16d = t_g16_last[0]

        BAR3 = barrier()

        pe_fence3 = ("pe", P.counts.get("pe", 0))
        gamma_rep = scr_f32(4)
        t_grep = dma_sy("rep", gamma_rep,
                        bass.AP(g16_d.tensor, 0, [[0, 128], [1, E]]),
                        waits=(t_g16d, ("g16s", P.counts["g16s"])) + BAR3)

        IVA_XQTM, IVA_XQFM = (0, 1), (2, 3)
        xn_fs = [scr_f32(0), scr_f32(2)]
        b2_fs = [ex_f32(), ex_f32()]
        fworks = [scr_f32(6), scr_f32(8)]
        mmbank4 = [PMM[:, 0, :], PMM[:, 1, :], PKV[:, 0, :], PKV[:, 1, :]]

        stA2 = {}
        stB2 = {}
        stC2 = {}

        def IVa_A(i):
            xb = X2[:, i % 2, :]
            xn_f = xn_fs[i % 2]
            t_x = dma_ac("ds%d" % (i % 2), xb, x_d[i * 128:(i + 1) * 128, :],
                         waits=(free.get(("Ax", i % 2)),) + (BAR3 if i < 2 else ()))
            rstd, nb, t_nb, t_ss = ln_chain(xb, psum_dummy(0), t_x,
                                            free.get(("sqd", 0)))
            free[("sqd", 0)] = t_ss
            t_xnA = dv(lambda e, xb=xb, rstd=rstd, nb=nb, xn_f=xn_f:
                       e.tensor_scalar(xn_f, xb, rstd, nb, ALU.mult, ALU.add),
                       waits=(t_nb, free.get(("xnf", i % 2)))
                       + (BAR3 if i < 2 else ()))
            stA2[i] = t_xnA

        def IVa_B(i):
            t_xnA = stA2.pop(i)
            xb = X2[:, i % 2, :]
            xn_f = xn_fs[i % 2]
            b2_f = b2_fs[i % 2]
            ssn = stat(); lnr = stat(); irms = stat()
            amax = stat(); xsd = stat(); invd = stat()
            t_ssn = ac(lambda e, ssn=ssn, xn_f=xn_f: e.activation(
                psum_dummy(1), xn_f, AF.Square, accum_out=ssn),
                waits=(t_xnA, free.get(("sqd", 0))))
            free[("sqd", 0)] = t_ssn
            t_lnr = ac(lambda e, ssn=ssn, lnr=lnr: e.activation(
                lnr, ssn, AF.Ln, bias=1e-8, scale=INV_E), waits=(t_ssn,))
            t_irms = ac(lambda e, lnr=lnr, irms=irms: e.activation(
                irms, lnr, AF.Exp, scale=-0.5), waits=(t_lnr,))
            t_am = dv(lambda e, amax=amax, xn_f=xn_f: e.reduce_max(
                amax, xn_f, axis=AX.X, apply_absolute_value=True),
                waits=(t_xnA,))
            t_xs = dv(lambda e, amax=amax, irms=irms, i=i: e.tensor_scalar(
                XS16[:, i:i + 1], amax, irms, CONSTS[:, 6:7],
                ALU.mult, ALU.mult), waits=(t_am, t_irms))
            t_xr = ac(lambda e, irms=irms, xn_f=xn_f, b2_f=b2_f: e.activation(
                b2_f, xn_f, AF.Copy, scale=irms),
                waits=(t_irms, free.get(("b2f", 0)) or pe_fence3)
                + (BAR3 if i < 2 else ()))
            t_xsd = dv(lambda e, xsd=xsd, i=i: e.tensor_scalar(
                xsd, XS16[:, i:i + 1], 1e-8, None, ALU.add), waits=(t_xs,))
            t_invd = dv(lambda e, xsd=xsd, invd=invd: e.reciprocal(invd, xsd),
                        waits=(t_xsd,))
            t_zt = ac(lambda e, invd=invd, xb=xb, b2_f=b2_f: e.activation(
                xb, b2_f, AF.Copy, bias=MAGIC, scale=invd),
                waits=(t_invd, t_xr))
            free[("xnf", i % 2)] = t_xr
            t_c1 = dv(lambda e, xb=xb, b2_f=b2_f: e.tensor_scalar(
                b2_f, xb, MAGIC, -127.0, ALU.subtract, ALU.max),
                waits=(t_zt,))
            free[("Ax", i % 2)] = t_c1
            t_xq = dv(lambda e, b2_f=b2_f, i=i: e.tensor_scalar(
                wa_bf(IVA_XQTM[i % 2]), b2_f, 127.0, None, ALU.min),
                waits=(t_c1, free.get(("xqtm", i % 2)) or pe_fence3))
            free[("b2f", 0)] = t_xq
            stB2[i] = (t_xq, t_xs)

        def IVa_C1(i):
            t_xq, t_xs = stB2.pop(i)
            fmslot = IVA_XQFM[i % 2]
            t_xqfm = transpose_tile(wa_bf(IVA_XQTM[i % 2]), wa_fm(fmslot), t_xq,
                                    free.get(("xqfm", i % 2)) or pe_fence3, free)
            free[("xqtm", i % 2)] = t_xqfm
            t_mms = []
            for j in range(4):

                def ffn_fn(e, j=j, fmslot=fmslot):
                    last = None
                    for c in range(KC):
                        last = e.matmul(mmbank4[j], wa_fm(fmslot)[:, c, :],
                                        WB[:, c, j * 512:(j + 1) * 512],
                                        start=(c == 0), stop=(c == KC - 1))
                    return last
                t_mm = pe(ffn_fn, waits=(t_xqfm, free.get(("bank", PHYS_A[j]))))
                t_mms.append(t_mm)
            free[("xqfm", i % 2)] = t_mms[-1]
            stC2[i] = (t_mms, t_xs)

        def IVa_C2(i):
            t_mms, t_xs = stC2.pop(i)
            fw = fworks[i % 2]
            ev_last = None
            for j in range(4):
                t_f1 = dv(lambda e, j=j, i=i, fw=fw: e.scalar_tensor_tensor(
                    out=fw[:, j * 512:(j + 1) * 512], in0=mmbank4[j],
                    scalar=XS16[:, i:i + 1],
                    in1=gamma_rep[:, j * 512:(j + 1) * 512],
                    op0=ALU.mult, op1=ALU.mult),
                    waits=(t_mms[j], t_xs, free.get(("fw", i % 2)),
                           (t_grep if i == 0 else None))
                    + (BAR3 if i == 0 else ()))
                free[("bank", PHYS_A[j])] = t_f1
                ev_last = t_f1
            t_out = dma_gd("oa%d" % (i % 2), out_d[i * 128:(i + 1) * 128, :],
                           fw,
                           waits=(ev_last, ("ox", P.counts["ox"])), accum=True)
            free[("fw", i % 2)] = t_out
            out_written.append(t_out)
            if i < 12:
                load_weight_chunk1("wta", WA, wo_d, 4 + i,
                                   waits=((pe_fence3,) if i == 0 else ()))

        out_written = []
        for k in range(NT + 3):
            if k >= 3:
                IVa_C2(k - 3)
            if 2 <= k <= NT + 1:
                IVa_C1(k - 2)
            if 1 <= k <= NT:
                IVa_B(k - 1)
            if k < NT:
                IVa_A(k)

        BAR4 = barrier()

        tok_wo_lo = load_weight_chunks("wta", WA, wo_d, 0, 4, waits=BAR4)
        pe_fence4 = ("pe", P.counts.get("pe", 0))
        gd_fence4 = ("gd", P.counts.get("gd", 0))
        lng_rep = scr_f32(6)
        lnb_rep = scr_f32(8)
        t_lgrep = dma_sy("rep", lng_rep,
                         bass.AP(lng_d.tensor, 0, [[0, 128], [1, E]]),
                         waits=BAR4)
        t_lbrep = dma_sy("rep", lnb_rep,
                         bass.AP(lnb_d.tensor, 0, [[0, 128], [1, E]]),
                         waits=BAR4)

        B_QIN, B_QFM, B_UIN, B_S, B_SFM = (0, 1), (2, 3), (4, 5), (6, 7), (8, 9)
        out_tms = [scr_f32(0), scr_f32(2)]
        t1_fs = [scr_f32(4), wb_f32(10)]
        attn_fs = [wb_f32(12), wb_f32(14)]
        psum_dummy_b = SQD[:, :]

        stF = {}
        stG = {}
        stH = {}

        def IVb_F(i):
            t_qi = dma_sy("ds%d" % (2 + i % 2), wb_bf(B_QIN[i % 2]), qsp(i),
                          waits=(free.get(("Bq", i % 2)),) + (BAR4 if i < 2 else ()))
            spall4 = ((("sp2", P.counts["sp2"]), ("sp3", P.counts["sp3"]))
                      if i < 2 else ())
            t_ui = dma_sy("ds%d" % (4 + i % 2), wb_bf(B_UIN[i % 2]), usp(i),
                          waits=(free.get(("Bu", i % 2)),) + spall4
                          + (BAR4 if i < 2 else ()))
            fmslot = B_QFM[i % 2]
            t_qfm = transpose_tile(wb_bf(B_QIN[i % 2]), wb_fm(fmslot), t_qi,
                                   free.get(("Bqfm", i % 2)) or pe_fence4, free,
                                   evac="ac")
            free[("Bq", i % 2)] = t_qfm

            out_tm = out_tms[i % 2]

            def oe_fn(e, fmslot=fmslot, half=0):
                last = None
                qfm = wb_fm(fmslot)
                for h in range(half * 8, half * 8 + 8):
                    last = e.matmul(
                        PKV[:, (h // 4) % 2, (h % 4) * 128:(h % 4 + 1) * 128],
                        qfm[:, h, :], KVBF[:, h * 128:(h + 1) * 128],
                        start=True, stop=True)
                return last
            t_oe1 = pe(lambda e, fmslot=fmslot: oe_fn(e, fmslot, 0),
                       waits=(t_qfm, free.get(("pkv", 0)),
                              (t_kvbf if i == 0 else None))
                       + (BAR4 if i == 0 else ()))
            t_oev1 = ac(lambda e, out_tm=out_tm: [
                e.activation(out_tm[:, b * 512:(b + 1) * 512], PKV[:, b, :],
                             AF.Copy)
                for b in range(2)][-1],
                waits=(t_oe1, free.get(("outtm", i % 2)))
                + (BAR4 if i < 2 else ()))
            t_oe2 = pe(lambda e, fmslot=fmslot: oe_fn(e, fmslot, 1),
                       waits=(t_oev1,))
            free[("Bqfm", i % 2)] = t_oe2
            t_oev = ac(lambda e, out_tm=out_tm: [
                e.activation(out_tm[:, 1024 + b * 512:1024 + (b + 1) * 512],
                             PKV[:, b, :], AF.Copy)
                for b in range(2)][-1],
                waits=(t_oe2,))
            free[("pkv", 0)] = t_oev
            stF[i] = (t_oev, t_ui)

        def IVb_G(i):
            t_oev, t_ui = stF.pop(i)
            out_tm = out_tms[i % 2]
            rstd, nb, t_nb, t_ss = ln_chain(out_tm, psum_dummy_b, t_oev,
                                            free.get(("sqd", 0)))
            free[("sqd", 0)] = t_ss
            t1_f = t1_fs[i % 2]
            t_t1 = dv(lambda e, rstd=rstd, nb=nb, out_tm=out_tm, t1_f=t1_f:
                      e.tensor_scalar(t1_f, out_tm, rstd, nb, ALU.mult, ALU.add),
                      waits=(t_nb, free.get(("t1f", i % 2)))
                      + (BAR4 if i < 2 else ()) + ((pe_fence4,) if i < 2 else ()))
            t_p1 = dv(lambda e, out_tm=out_tm, t1_f=t1_f: e.tensor_mul(
                out_tm, t1_f, lng_rep),
                waits=(t_t1, (t_lbrep if i == 0 else None)))
            free[("outtm", i % 2)] = t_p1
            t_p2 = dv(lambda e, out_tm=out_tm, t1_f=t1_f: e.tensor_add(
                t1_f, out_tm, lnb_rep),
                waits=(t_p1, (t_lbrep if i == 0 else None)))
            pass
            sslot = B_S[i % 2]
            t_s = dv(lambda e, sslot=sslot, t1_f=t1_f, i=i: e.tensor_mul(
                wb_bf(sslot), t1_f, wb_bf(B_UIN[i % 2])),
                waits=(t_p2, t_ui, free.get(("Bs", i % 2)) or pe_fence4))
            free[("Bu", i % 2)] = t_s
            free[("t1f", i % 2)] = t_s
            stG[i] = (t_s, sslot)

        def IVb_H1(i):
            t_s, sslot = stG.pop(i)
            sfm = B_SFM[i % 2]
            t_sfm = transpose_tile(wb_bf(sslot), wb_fm(sfm), t_s,
                                   free.get(("Bsfm", i % 2)) or pe_fence4, free,
                                   evac="ac")
            free[("Bs", i % 2)] = t_sfm
            attn_f = attn_fs[i % 2]
            WO_BANKS = [PMM[:, 0, :], PMM[:, 1, :], PKV[:, 2, :], PKV[:, 3, :]]
            WO_PHYS = (0, 1, 6, 7)
            t_mms = []
            for j in range(4):
                t_mm = proj_mm(WO_BANKS[j], wb_fm(sfm), WA, j, 3, range(KC),
                               waits=(t_sfm, free.get(("bank", WO_PHYS[j])),
                                      (tok_wo_lo if i == 0 and j == 0 else None)))
                t_mms.append(t_mm)
            free[("Bsfm", i % 2)] = t_mms[-1]
            stH[i] = t_mms

        def IVb_H2(i):
            t_mms = stH.pop(i)
            attn_f = attn_fs[i % 2]
            WO_BANKS = [PMM[:, 0, :], PMM[:, 1, :], PKV[:, 2, :], PKV[:, 3, :]]
            WO_PHYS = (0, 1, 6, 7)
            ev_last = None
            for j in range(4):
                t_ev = dv(lambda e, j=j, attn_f=attn_f: e.tensor_copy(
                    attn_f[:, j * 512:(j + 1) * 512], WO_BANKS[j]),
                    waits=(t_mms[j], free.get(("attnf", i % 2)))
                    + ((pe_fence4,) if i < 2 and j == 0 else ()))
                free[("bank", WO_PHYS[j])] = t_ev
                ev_last = t_ev
            t_acc = dma_gd("ob%d" % (i % 2), out_d[i * 128:(i + 1) * 128, :],
                           attn_f,
                           waits=(ev_last, out_written[i]), accum=True)
            free[("attnf", i % 2)] = t_acc

        for k in range(NT + 3):
            if k >= 3:
                IVb_H2(k - 3)
            if 2 <= k <= NT + 1:
                IVb_H1(k - 2)
            if 1 <= k <= NT:
                IVb_G(k - 1)
            if k < NT:
                IVb_F(k)

        finals = [(s, P.counts.get(s, 0)) for s in ("oa0", "oa1", "ob0", "ob1")]

        with nc.Block() as block:
            def mk(eng_name):
                def body(eng):
                    satisfied = {}
                    for fn, waits, inc, amt in P.steps[eng_name]:
                        for (s, v) in waits:
                            if satisfied.get(s, 0) >= v:
                                continue
                            eng.wait_ge(sems[s], v)
                            satisfied[s] = v
                        inst = fn(eng)
                        if inc is not None:
                            inst.then_inc(sems[inc], amt)
                    if eng_name == "gpsimd":
                        for s, v in finals:
                            eng.wait_ge(sems[s], v)
                return body

            block.sync(mk("sync"))
            block.gpsimd(mk("gpsimd"))
            block.tensor(mk("tensor"))
            block.vector(mk("vector"))
            block.scalar(mk("scalar"))

    return nc


_NC_CACHE = None


def kernel(L0_attn_x, L0_attn_Wqk, L0_attn_bqk, L0_attn_Wv, L0_attn_bv,
           L0_attn_Wu, L0_attn_bu, L0_attn_Wo, L0_attn_bo,
           L0_attn_ln_g, L0_attn_ln_b, L0_ffn_weight, L0_ffn_bias):
    global _NC_CACHE
    x = np.ascontiguousarray(np.asarray(L0_attn_x, dtype=np.float32))
    B, N, E_ = x.shape
    xf = x.reshape(B * N, E_)

    def arr(a):
        return np.ascontiguousarray(np.asarray(a, dtype=np.float32))

    def row(a):
        return arr(a).reshape(1, E_)

    consts = np.zeros((1, 16), np.float32)
    for i, v in enumerate(REG_CONSTS):
        consts[0, i] = v

    common = {
        "wqk": arr(L0_attn_Wqk), "wv": arr(L0_attn_Wv), "wu": arr(L0_attn_Wu),
        "wo": arr(L0_attn_Wo), "wf": arr(L0_ffn_weight),
        "bqk": row(L0_attn_bqk), "bv": row(L0_attn_bv), "bu": row(L0_attn_bu),
        "bo": row(L0_attn_bo), "fbias": row(L0_ffn_bias),
        "lng": row(L0_attn_ln_g), "lnb": row(L0_attn_ln_b),
        "cdb": consts,
        "sel": SEL_NP, "negm": NEGM_NP, "ident": IDENT_NP,
    }
    in_maps = []
    for c in range(8):
        m = dict(common)
        m["x"] = np.ascontiguousarray(xf[c * TOK:(c + 1) * TOK, :])
        in_maps.append(m)

    if _NC_CACHE is None:
        _NC_CACHE = build_nc()
    res = run_bass_kernel_spmd(_NC_CACHE, in_maps, core_ids=list(range(8)))
    out = np.concatenate([res.results[c]["out"] for c in range(8)], axis=0)
    return out.reshape(B, N, E_).astype(np.float32)


if __name__ == "__main__":
    nc = build_nc()
    print("built ok; instructions:",
          sum(1 for _ in nc.all_instructions()))


# version 21
# speedup vs baseline: 1.0301x; 1.0301x over previous
from contextlib import ExitStack

import ml_dtypes
import numpy as np

import concourse.bass as bass
import concourse.mybir as mybir
from concourse.bass_utils import run_bass_kernel_spmd

F32 = mybir.dt.float32
BF16 = mybir.dt.bfloat16
AF = mybir.ActivationFunctionType
ALU = mybir.AluOpType
AX = mybir.AxisListType

MAGIC = float(1.5 * 2 ** 23)
E = 2048
KC = 16
TOK = 2048
NT = 16
HEADS = 16
INV_E = 1.0 / 2048.0
REG_CONSTS = [1e-5, 1e-8, -1.0, MAGIC, -0.5, INV_E, 1.0 / 127.0]

ENGINES = ("sync", "gpsimd", "tensor", "vector", "scalar")

SEL_NP = np.zeros((10, 4 * 128), dtype=ml_dtypes.bfloat16)
for _b in range(4):
    SEL_NP[2 * _b:2 * _b + 2, _b * 128:(_b + 1) * 128] = 1
SEL_NP[8:10, 3 * 128:4 * 128] = 1
NEGM_NP = np.zeros((10, 1), np.float32)
NEGM_NP[1::2, 0] = -1.0
IDENT_NP = np.eye(128, dtype=ml_dtypes.bfloat16)


class Plan:
    def __init__(self):
        self.steps = {e: [] for e in ENGINES}
        self.counts = {}
        self.last = {}
        self.eng_dma = {}

    def emit(self, eng, fn, waits=(), inc=None, amt=1):
        tok = None
        if inc is not None:
            self.counts[inc] = self.counts.get(inc, 0) + amt
            tok = (inc, self.counts[inc])
            self.last[eng] = tok
            if amt == 16:
                self.eng_dma.setdefault(eng, {})[inc] = self.counts[inc]
        self.steps[eng].append(
            (fn, tuple(w for w in waits if w is not None), inc, amt))
        return tok


def build_nc():
    nc = bass.Bass("TRN2", target_bir_lowering=False)

    x_d = nc.dram_tensor("x", [TOK, E], F32, kind="ExternalInput").ap()
    wqk_d = nc.dram_tensor("wqk", [E, E], F32, kind="ExternalInput").ap()
    wv_d = nc.dram_tensor("wv", [E, E], F32, kind="ExternalInput").ap()
    wu_d = nc.dram_tensor("wu", [E, E], F32, kind="ExternalInput").ap()
    wo_d = nc.dram_tensor("wo", [E, E], F32, kind="ExternalInput").ap()
    wf_d = nc.dram_tensor("wf", [E, E], F32, kind="ExternalInput").ap()
    bqk_d = nc.dram_tensor("bqk", [1, E], F32, kind="ExternalInput").ap()
    bv_d = nc.dram_tensor("bv", [1, E], F32, kind="ExternalInput").ap()
    bu_d = nc.dram_tensor("bu", [1, E], F32, kind="ExternalInput").ap()
    bo_d = nc.dram_tensor("bo", [1, E], F32, kind="ExternalInput").ap()
    fbias_d = nc.dram_tensor("fbias", [1, E], F32, kind="ExternalInput").ap()
    lng_d = nc.dram_tensor("lng", [1, E], F32, kind="ExternalInput").ap()
    lnb_d = nc.dram_tensor("lnb", [1, E], F32, kind="ExternalInput").ap()
    cdb_d = nc.dram_tensor("cdb", [1, 16], F32, kind="ExternalInput").ap()
    sel_d = nc.dram_tensor("sel", [10, 4 * 128], BF16, kind="ExternalInput").ap()
    negm_d = nc.dram_tensor("negm", [10, 1], F32, kind="ExternalInput").ap()
    ident_d = nc.dram_tensor("ident", [128, 128], BF16, kind="ExternalInput").ap()
    out_d = nc.dram_tensor("out", [TOK, E], F32, kind="ExternalOutput").ap()
    xnsp_d = nc.dram_tensor("xnsp", [128, NT * KC * 128], BF16).ap()
    qsp_d = nc.dram_tensor("qsp", [128, NT * E], BF16).ap()
    usp_d = nc.dram_tensor("usp", [128, NT * E], BF16).ap()
    g16_d = nc.dram_tensor("g16d", [1, E], F32).ap()
    cci_d = nc.dram_tensor("cci", [128, HEADS * 128], F32).ap()
    cco_d = nc.dram_tensor("cco", [128, HEADS * 128], F32).ap()

    def xnsp(i):
        return xnsp_d[:, i * KC * 128:(i + 1) * KC * 128]

    def qsp(i):
        return qsp_d[:, i * E:(i + 1) * E]

    def usp(i):
        return usp_d[:, i * E:(i + 1) * E]

    with ExitStack() as ctx:
        ec = ctx.enter_context
        WA = ec(nc.sbuf_tensor("WA", [128, KC, E], BF16))
        WB = ec(nc.sbuf_tensor("WB", [128, KC, E], BF16))
        SCR = ec(nc.sbuf_tensor("SCR", [128, 10, E], BF16))
        X2 = ec(nc.sbuf_tensor("X2", [128, 2, E], F32))
        EXTRA = ec(nc.sbuf_tensor("EXTRA", [128, 2, E], BF16))
        BIASBF = ec(nc.sbuf_tensor("BIASBF", [10, E], BF16))
        SEL = ec(nc.sbuf_tensor("SEL", [10, 4 * 128], BF16))
        NEGM = ec(nc.sbuf_tensor("NEGM", [10, 1], F32))
        KVBF = ec(nc.sbuf_tensor("KVBF", [128, HEADS * 128], BF16))
        IDENT = ec(nc.sbuf_tensor("IDENT", [128, 128], BF16))
        STATS = ec(nc.sbuf_tensor("STATS", [128, 576], F32))
        G16 = ec(nc.sbuf_tensor("G16", [128, KC], F32))
        XS16 = ec(nc.sbuf_tensor("XS16", [128, NT], F32))
        CONSTS = ec(nc.sbuf_tensor("CONSTS", [128, 16], F32))
        SQD = ec(nc.sbuf_tensor("SQD", [128, E], BF16))
        PMM = ec(nc.psum_tensor("PMM", [128, 2, 512], F32))
        PTR = ec(nc.psum_tensor("PTR", [128, 2, 1024], BF16))
        PKV = ec(nc.psum_tensor("PKV", [128, 4, 512], F32))

        sems = {}
        for s in ("pe", "dv", "ac", "cc", "init", "rep", "g16s",
                  "wta", "wtA1", "wtA2", "wtA3", "wtA4", "wtA5", "wtb", "wtc", "ox", "oxg", "ccd", "ccd2", "wf0", "wf1", "wf2", "wf3",
                  "bsy", "bgd", "bac",
                  "ds0", "ds1", "ds2", "ds3", "ds4", "ds5",
                  "sp0", "sp1", "sp2", "sp3",
                  "oa0", "oa1", "ob0", "ob1"):
            sems[s] = ec(nc.semaphore(s))

        for i, v in enumerate(REG_CONSTS):
            nc.const_aps.register(F32, v, CONSTS[:, i:i + 1])

        P = Plan()

        def dma_sy(sem, out, in_, waits=()):
            return P.emit("sync", lambda e, o=out, i=in_: e.dma_start(out=o, in_=i),
                          waits=waits, inc=sem, amt=16)

        def dma_gd(sem, out, in_, waits=(), accum=False):
            if accum:
                return P.emit(
                    "gpsimd",
                    lambda e, o=out, i=in_: e.dma_start(out=o, in_=i,
                                                        accum_op=ALU.add),
                    waits=waits, inc=sem, amt=16)
            return P.emit("gpsimd",
                          lambda e, o=out, i=in_: e.dma_start(out=o, in_=i),
                          waits=waits, inc=sem, amt=16)

        def pe(fn, waits=()):
            return P.emit("tensor", fn, waits=waits, inc="pe")

        def dv(fn, waits=()):
            return P.emit("vector", fn, waits=waits, inc="dv")

        def ac(fn, waits=()):
            return P.emit("scalar", fn, waits=waits, inc="ac")

        def barrier():
            toks = []
            for eng, bsem in (("sync", "bsy"), ("gpsimd", "bgd"),
                              ("scalar", "bac")):
                dws = tuple((s, v) for s, v in
                            P.eng_dma.get(eng, {}).items())
                toks.append(P.emit(eng, lambda e: e.nop(),
                                   waits=dws, inc=bsem))
            for eng in ("tensor", "vector"):
                if P.last.get(eng):
                    toks.append(P.last[eng])
            if P.last.get("scalar") and P.last["scalar"][0] == "ac":
                toks.append(P.last["scalar"])
            return tuple(toks)

        def dma_ac(sem, out, in_, waits=()):
            return P.emit("scalar", lambda e, o=out, i=in_: e.dma_start(out=o, in_=i),
                          waits=waits, inc=sem, amt=16)

        def scr_bf(i):
            return SCR[:, i, :]

        def scr_f32(i):
            return SCR[:, i:i + 2, :].bitcast(F32).rearrange("p a b -> p (a b)")

        def wa_bf(i):
            return WA[:, i, :]

        def wb_bf(i):
            return WB[:, i, :]

        def wa_fm(i):
            return WA[:, i, :].rearrange("p (c t) -> p c t", c=KC)

        def wb_fm(i):
            return WB[:, i, :].rearrange("p (c t) -> p c t", c=KC)

        def wa_f32(i):
            return WA[:, i:i + 2, :].bitcast(F32).rearrange("p a b -> p (a b)")

        def wb_f32(i):
            return WB[:, i:i + 2, :].bitcast(F32).rearrange("p a b -> p (a b)")

        def ex_bf(i):
            return EXTRA[:, i, :]

        def ex_f32():
            return EXTRA[:, 0:2, :].bitcast(F32).rearrange("p a b -> p (a b)")

        def psum_dummy(i):
            return SQD[:, :]

        def load_weight_chunks(sem, wdst, wsrc, c0, c1, waits=()):
            src = wsrc.rearrange("(c p) m -> p c m", p=128)[:, c0:c1, :]
            return dma_gd(sem, wdst[:, c0:c1, :], src, waits=waits)

        def load_weight_chunk1(sem, wdst, wsrc, c, waits=()):
            src = wsrc.rearrange("(c p) m -> p c m", p=128)[:, c:c + 1, :]
            return dma_gd(sem, wdst[:, c:c + 1, :], src, waits=waits)

        st_col = [0]

        def stat(n=1):
            c = st_col[0]
            st_col[0] += n
            assert st_col[0] <= 576
            return STATS[:, c:c + n]

        def transpose_tile(src_bf, dst_fm, src_tok, dst_free_tok, free,
                           evac="dv"):
            last = None
            emitf = dv if evac == "dv" else ac
            for h in range(2):
                t_tr = pe(lambda e, h=h, s=src_bf: [
                    e.transpose(PTR[:, h, c * 128:(c + 1) * 128],
                                s[:, (h * 8 + c) * 128:(h * 8 + c + 1) * 128],
                                IDENT[:])
                    for c in range(8)][-1],
                    waits=(src_tok, free.get(("ptr", h))))
                if evac == "dv":
                    t_ev = dv(lambda e, h=h, d=dst_fm: e.tensor_copy(
                        d[:, h * 8:(h + 1) * 8, :].rearrange("p c t -> p (c t)"),
                        PTR[:, h, :]),
                        waits=(t_tr, dst_free_tok))
                else:
                    t_ev = ac(lambda e, h=h, d=dst_fm: e.activation(
                        d[:, h * 8:(h + 1) * 8, :].rearrange("p c t -> p (c t)"),
                        PTR[:, h, :], AF.Copy),
                        waits=(t_tr, dst_free_tok))
                free[("ptr", h)] = t_ev
                last = t_ev
            return last

        def ln_chain(xb, sq_dst, t_x, sq_free, extra=()):
            s1 = stat(); ss = stat(); a = stat(); vp = stat()
            lnv = stat(); rstd = stat(); nb = stat()
            t_s1 = dv(lambda e, xb=xb, s1=s1: e.reduce_sum(s1, xb, axis=AX.X),
                      waits=(t_x,))
            t_ss = ac(lambda e, xb=xb, ss=ss, d=sq_dst: e.activation(
                d, xb, AF.Square, accum_out=ss), waits=(t_x, sq_free))
            t_a = dv(lambda e, s1=s1, a=a: e.scalar_tensor_tensor(
                out=a, in0=s1, scalar=-INV_E, in1=s1,
                op0=ALU.mult, op1=ALU.mult), waits=(t_s1,))
            t_vp = dv(lambda e, a=a, ss=ss, vp=vp: e.tensor_add(vp, a, ss),
                      waits=(t_a, t_ss))
            t_lnv = ac(lambda e, vp=vp, lnv=lnv: e.activation(
                lnv, vp, AF.Ln, bias=1e-5, scale=INV_E),
                waits=(t_vp,) + tuple(extra))
            t_rstd = ac(lambda e, lnv=lnv, rstd=rstd: e.activation(
                rstd, lnv, AF.Exp, scale=-0.5), waits=(t_lnv,))
            t_nb = dv(lambda e, s1=s1, rstd=rstd, nb=nb: e.scalar_tensor_tensor(
                out=nb, in0=s1, scalar=-INV_E, in1=rstd,
                op0=ALU.mult, op1=ALU.mult), waits=(t_rstd,))
            return rstd, nb, t_nb, t_ss

        free = {}

        tok_consts = dma_sy("init", CONSTS[:, :],
                            bass.AP(cdb_d.tensor, 0, [[0, 128], [1, 16]]))
        tok_ident = dma_sy("init", IDENT[:, :], ident_d)
        tok_sel = dma_sy("init", SEL[:, :], sel_d)
        tok_negm = dma_sy("init", NEGM[:, :], negm_d)

        bstage = SCR[0:10, 8:10, :].bitcast(F32).rearrange("p a b -> p (a b)")
        for b, bd in enumerate((bqk_d, bv_d, bu_d, bo_d, fbias_d)):
            dma_sy("init",
                   SCR[2 * b:2 * b + 2, 8:10, :].bitcast(F32)
                   .rearrange("p a b -> p (a b)"),
                   bass.AP(bd.tensor, 0, [[0, 2], [1, E]]))
        init_all = ("init", P.counts["init"])
        bias_toks = {}

        def emit_bias_build():
            t_hi = dv(lambda e: e.tensor_copy(BIASBF[:, :], bstage),
                      waits=(init_all,))
            bias_toks["lo"] = dv(lambda e: e.scalar_tensor_tensor(
                out=BIASBF[:, :], in0=BIASBF[:, :], scalar=NEGM[:, :],
                in1=bstage, op0=ALU.mult, op1=ALU.add),
                waits=(t_hi,))
        tok_wa_pairs = [load_weight_chunks("wta", WA, wqk_d, 0, 2),
                        load_weight_chunks("wtA1", WA, wqk_d, 2, 4),
                        load_weight_chunks("wtA2", WA, wqk_d, 4, 6),
                        load_weight_chunks("wtA3", WA, wqk_d, 6, 8),
                        load_weight_chunks("wtA4", WA, wqk_d, 12, 14),
                        load_weight_chunks("wtA5", WA, wqk_d, 14, 16)]
        wl_toks = [dma_sy("ds4", scr_f32(4), wqk_d[8 * 128:9 * 128, :]),
                   dma_sy("ds5", scr_f32(6), wqk_d[9 * 128:10 * 128, :])]
        wa_cast_toks = []

        def emit_wa_cast(c):
            stg8 = scr_f32(4 if c % 2 == 0 else 6)
            wa_cast_toks.append(dv(
                lambda e, s=stg8, c=c: e.tensor_copy(WA[:, c, :], s),
                waits=(wl_toks[c - 8],)))

        PTRF = [PTR[:, h, :].bitcast(F32) for h in range(2)]
        BANKS_A = [PMM[:, 0, :], PMM[:, 1, :], PKV[:, 0, :], PKV[:, 1, :]]
        BANKS_B = [PMM[:, 0, :], PMM[:, 1, :], PTRF[0], PTRF[1]]
        PHYS_A = (0, 1, 4, 5)
        PHYS_B = (0, 1, 2, 3)

        def proj_mm(bank_ap, fm, W, j, brow, ks, waits):
            def fn(e, bank_ap=bank_ap, fm=fm, W=W, j=j, brow=brow, ks=tuple(ks)):
                for idx, c in enumerate(ks):
                    e.matmul(bank_ap, fm[:, c, :],
                             W[:, c, j * 512:(j + 1) * 512],
                             start=(idx == 0), stop=False)
                return e.matmul(bank_ap,
                                SEL[:, brow * 128:(brow + 1) * 128],
                                BIASBF[:, j * 512:(j + 1) * 512],
                                start=False, stop=True)
            return pe(fn, waits=waits)

        free[("ptr", 0)] = init_all
        free[("ptr", 1)] = init_all
        I_XNFM, I_QTM = (1, 2), (3, 4)
        elu_scr = [scr_f32(0), scr_f32(2)]
        outx_toks = []
        wv_pre = [0, 5] + list(range(6, KC))
        korder2 = wv_pre[2:] + wv_pre[:2] + [1, 2, 3, 4]
        korder2 = list(korder2)

        stA = {}
        stB = {}

        def I_A(i):
            xb = X2[:, i % 2, :]
            t_x = dma_ac("ds%d" % (i % 2), xb, x_d[i * 128:(i + 1) * 128, :],
                         waits=(free.get(("x", i % 2)),))
            rstd, nb, t_nb, t_ss = ln_chain(xb, psum_dummy(0), t_x,
                                            free.get(("sqd", 0)),
                                            extra=((init_all,) if i < 2 else ()))
            free[("sqd", 0)] = t_ss
            t_xnbf = dv(lambda e, xb=xb, rstd=rstd, nb=nb, i=i: e.tensor_scalar(
                ex_bf(i % 2), xb, rstd, nb, ALU.mult, ALU.add),
                waits=(t_nb, free.get(("xnbf", i % 2))))
            free[("x", i % 2)] = t_xnbf
            stA[i] = t_xnbf

        def I_B(i):
            t_xnbf = stA.pop(i)
            fmslot = I_XNFM[i % 2]
            t_xnfm = transpose_tile(ex_bf(i % 2), wb_fm(fmslot), t_xnbf,
                                    free.get(("xnfm", i % 2)), free)
            free[("xnbf", i % 2)] = t_xnfm
            t_sp_xn = dma_gd("sp%d" % (i % 2), xnsp(i), wb_bf(fmslot),
                             waits=(t_xnfm,))
            stB[i] = (t_xnfm, t_sp_xn)

        def I_C(i):
            t_xnfm, t_sp_xn = stB.pop(i)
            fmslot = I_XNFM[i % 2]
            qslot = I_QTM[i % 2]
            t_q_last = None
            for j in range(4):
                bank_ap = BANKS_A[j]
                pb = PHYS_A[j]
                if i == 0:
                    for ci, c in enumerate(range(KC)):
                        wv = (tok_wa_pairs[c // 2] if c < 8
                              else (wa_cast_toks[c - 8] if c < 12
                                    else tok_wa_pairs[4 + (c - 12) // 2]))
                        pe(lambda e, bank_ap=bank_ap, fmslot=fmslot, c=c, ci=ci, j=j:
                           e.matmul(bank_ap, wb_fm(fmslot)[:, c, :],
                                    WA[:, c, j * 512:(j + 1) * 512],
                                    start=(ci == 0), stop=False),
                           waits=(t_xnfm, free.get(("bank", pb)), wv))
                    t_mm = pe(lambda e, bank_ap=bank_ap, j=j: e.matmul(
                        bank_ap, SEL[:, 0:128],
                        BIASBF[:, j * 512:(j + 1) * 512],
                        start=False, stop=True),
                        waits=((bias_toks["lo"],) if j == 0 else ()))
                else:
                    t_mm = proj_mm(bank_ap, wb_fm(fmslot), WA, j, 0, range(KC),
                                   waits=(t_xnfm, free.get(("bank", pb)),
                                          (tok_wa if i == 1 else None))
                                   + (tuple(tok_wa_pairs) if i == 1 else ()))
                eslot = j % 2
                ebuf = elu_scr[eslot][:, 0:512]
                rbuf = elu_scr[eslot][:, 512:1024]
                sbuf = elu_scr[eslot][:, 1024:1536]
                t_e = ac(lambda e, bank_ap=bank_ap, eb=ebuf: e.activation(
                    eb, bank_ap, AF.Exp),
                    waits=(t_mm, free.get(("elu", eslot))))
                t_r = ac(lambda e, bank_ap=bank_ap, rb=rbuf: e.activation(
                    rb, bank_ap, AF.Relu), waits=(t_mm,))
                t_s = dv(lambda e, eb=ebuf, rb=rbuf, sb=sbuf: e.scalar_tensor_tensor(
                    out=sb, in0=eb, scalar=1.0, in1=rb, op0=ALU.min, op1=ALU.add),
                    waits=(t_e, t_r, free.get(("elus", eslot))))
                free[("bank", pb)] = t_r
                t_q = ac(lambda e, sb=sbuf, j=j, qslot=qslot: e.activation(
                    wb_bf(qslot)[:, j * 512:(j + 1) * 512], sb, AF.Copy, bias=-1.0),
                    waits=(t_s, free.get(("qtm", i % 2))))
                free[("elu", eslot)] = t_q
                free[("elus", eslot)] = t_q
                t_q_last = t_q
            t_sp_q = dma_gd("sp%d" % (i % 2), qsp(i), wb_bf(qslot),
                            waits=(t_q_last,))
            free[("xnfm", i % 2)] = t_sp_q
            free[("qtm", i % 2)] = t_sp_q
            if i < len(wv_pre):
                tok_wv_hi = load_weight_chunk1("wtb", WB, wv_d, wv_pre[i])
            if i == 14:
                for qn in (0, 1):
                    outx_toks.append(dma_ac(
                        "ox", out_d[qn * 512:(qn + 1) * 512, :],
                        x_d[qn * 512:(qn + 1) * 512, :]))

        tok_wa = None
        for k in range(NT + 2):
            if 1 <= k <= NT:
                I_B(k - 1)
            if k < NT:
                I_A(k)
            if k == 0:
                emit_wa_cast(8)
            if k == 1:
                emit_bias_build()
                emit_wa_cast(9)
                wl_toks.append(dma_sy("ds4", scr_f32(4),
                                      wqk_d[10 * 128:11 * 128, :],
                                      waits=(wa_cast_toks[0],)))
            if k == 2:
                emit_wa_cast(10)
                wl_toks.append(dma_sy("ds5", scr_f32(6),
                                      wqk_d[11 * 128:12 * 128, :],
                                      waits=(wa_cast_toks[1],)))
                emit_wa_cast(11)
                tok_wa = wa_cast_toks[-1]
            if k >= 2:
                I_C(k - 2)

        BAR1 = barrier()

        tok_wv_lo = load_weight_chunks("wtc", WB, wv_d, 1, 5, waits=BAR1)
        pe_fence1 = ("pe", P.counts.get("pe", 0))

        II_XNFM, II_QIN, II_VTM = (0, 1), (2, 3), 4
        kv_last = None
        tok_wu_last = None
        for i in range(NT):
            fm = SCR[:, II_XNFM[i % 2], :].rearrange("p (c t) -> p c t", c=KC)
            spall = ((("sp0", P.counts["sp0"]), ("sp1", P.counts["sp1"]))
                     if i < 2 else ())
            t_xn = dma_sy("ds%d" % (2 + i % 2), SCR[:, II_XNFM[i % 2], :], xnsp(i),
                          waits=(free.get(("IIxn", i % 2)),) + spall
                          + (BAR1 if i < 2 else ()))
            t_qi = dma_sy("ds%d" % (4 + i % 2), scr_bf(II_QIN[i % 2]), qsp(i),
                          waits=(free.get(("IIq", i % 2)),) + spall
                          + (BAR1 if i < 2 else ()))
            t_v_last = None
            for j in range(4):
                bank_ap = BANKS_B[j]
                pb = PHYS_B[j]
                if i == 0:
                    for ci, c in enumerate(korder2):
                        wv = (tok_wv_lo if c in (1, 2, 3, 4)
                              else ("wtb", 16 * 12))
                        pe(lambda e, bank_ap=bank_ap, fm=fm, c=c, ci=ci, j=j:
                           e.matmul(bank_ap, fm[:, c, :],
                                    WB[:, c, j * 512:(j + 1) * 512],
                                    start=(ci == 0), stop=False),
                           waits=(t_xn, free.get(("bank", pb)), wv))
                    t_mm = pe(lambda e, bank_ap=bank_ap, j=j: e.matmul(
                        bank_ap, SEL[:, 128:256],
                        BIASBF[:, j * 512:(j + 1) * 512],
                        start=False, stop=True), waits=())
                else:
                    t_mm = proj_mm(bank_ap, fm, WB, j, 1, korder2,
                                   waits=(t_xn, free.get(("bank", pb))))
                t_v = ac(lambda e, j=j, bank_ap=bank_ap: e.activation(
                    scr_bf(II_VTM)[:, j * 512:(j + 1) * 512], bank_ap,
                    AF.Silu), waits=(t_mm, free.get(("vtm", 0))))
                free[("bank", pb)] = t_v
                t_v_last = t_v

            def kv_fn(e, i=i, qs=II_QIN[i % 2]):
                last = None
                for h in range(HEADS):
                    last = e.matmul(
                        PKV[:, h // 4, (h % 4) * 128:(h % 4 + 1) * 128],
                        scr_bf(qs)[:, h * 128:(h + 1) * 128],
                        scr_bf(II_VTM)[:, h * 128:(h + 1) * 128],
                        start=(i == 0 and h % 4 == 0),
                        stop=(i == NT - 1 and h % 4 == 3))
                return last
            t_kv = pe(kv_fn, waits=(t_qi, t_v_last))
            kv_last = t_kv
            free[("vtm", 0)] = t_kv
            free[("IIxn", i % 2)] = t_kv
            free[("IIq", i % 2)] = t_kv
            if i < 8:
                for c in (korder2[2 * i], korder2[2 * i + 1]):
                    tok_wu_last = load_weight_chunk1(
                        "wta", WA, wu_d, c,
                        waits=((pe_fence1,) if i == 0 else ()))

        wf_toks = {}
        wf_toks[0] = dma_gd("wf0", X2[:, 1, :], wf_d[0:128, :],
                            waits=(free.get(("x", 1)),))
        wf_toks[1] = dma_gd("wf1", scr_f32(8), wf_d[128:256, :],
                            waits=(bias_toks["lo"],))
        wf_toks[2] = dma_gd("wf2", scr_f32(4), wf_d[256:384, :],
                            waits=(kv_last,))
        wf_toks[3] = dma_gd("wf3", scr_f32(6), wf_d[384:512, :],
                            waits=(tok_wa,))
        kvsb = X2[:, 0, :]
        t_kvev = dv(lambda e: [e.tensor_copy(kvsb[:, b * 512:(b + 1) * 512],
                                             PKV[:, b, :]) for b in range(4)][-1],
                    waits=(kv_last,))

        BAR2 = barrier()

        III_XNFM = (0, 1)
        utm_bufs = [scr_bf(2), ex_bf(1)]
        wtnk = [scr_bf(3), ex_bf(0)]
        stage_f = [X2[:, 1, :], scr_f32(8), scr_f32(4), scr_f32(6)]
        MAGICR = CONSTS[:, 3:4]
        NEG1R = CONSTS[:, 2:3]

        t_g16d = None
        t_cci = None
        t_kvbf = None
        stP1 = {}
        stP2 = {}

        def III_wf(c):
            wf_toks[c] = dma_gd("wf%d" % (c % 4), stage_f[c % 4],
                                wf_d[c * 128:(c + 1) * 128, :],
                                waits=(free.get(("stage", c % 4)),))

        def III_P1(c):
            stg = stage_f[c % 4]
            t_wf = wf_toks.pop(c)
            red = stat(); ge = stat(); ginv = stat()
            t_red = dv(lambda e, stg=stg, red=red: e.reduce_sum(
                red, stg, axis=AX.X, apply_absolute_value=True), waits=(t_wf,))
            t_g = dv(lambda e, red=red, c=c: e.tensor_scalar(
                G16[:, c:c + 1], red, INV_E, None, ALU.mult), waits=(t_red,))
            t_ge = dv(lambda e, ge=ge, c=c: e.tensor_scalar(
                ge, G16[:, c:c + 1], 1e-8, None, ALU.add), waits=(t_g,))
            t_gi = dv(lambda e, ge=ge, ginv=ginv: e.reciprocal(ginv, ge),
                      waits=(t_ge,))
            stP1[c] = (t_gi, ginv, t_g)

        def III_P2(c):
            t_gi, ginv, t_g = stP1.pop(c)
            stg = stage_f[c % 4]
            t_d1 = dv(lambda e, stg=stg, ginv=ginv: e.tensor_scalar(
                stg, stg, ginv, MAGICR, ALU.mult, ALU.add), waits=(t_gi,))
            t_c1 = dv(lambda e, stg=stg: e.tensor_scalar(
                stg, stg, MAGICR, NEG1R, ALU.subtract, ALU.max),
                waits=(t_d1,))
            t_c2 = dv(lambda e, stg=stg, c=c: e.tensor_scalar(
                wtnk[c % 2], stg, 1.0, None, ALU.min),
                waits=(t_c1, free.get(("wtnk", c % 2))))
            free[("stage", c % 4)] = t_c2
            t_g16c = dma_gd("g16s", g16_d[0:1, c * 128:(c + 1) * 128],
                            G16[:, c:c + 1], waits=(t_g,))
            stP2[c] = (t_c2, t_g16c)

        t_g16_last = [None]

        def III_P3(c):
            t_c2, t_g16c = stP2.pop(c)
            t_g16_last[0] = t_g16c
            last = None
            for h in range(2):
                t_tr = pe(lambda e, h=h, c=c: [
                    e.transpose(PTR[:, h, k * 128:(k + 1) * 128],
                                wtnk[c % 2][:, (h * 8 + k) * 128:(h * 8 + k + 1) * 128],
                                IDENT[:])
                    for k in range(8)][-1],
                    waits=(t_c2, free.get(("ptr", h))))
                t_ev = dv(lambda e, h=h, c=c: e.tensor_copy(
                    WB[:, h * 8:(h + 1) * 8, c * 128:(c + 1) * 128],
                    PTR[:, h, :].rearrange("p (a b) -> p a b", a=8)),
                    waits=(t_tr,) + (BAR2 if c <= 1 else ()))
                free[("ptr", h)] = t_ev
                last = t_ev
            free[("wtnk", c % 2)] = last
            return t_g16c

        III_P1(0)
        III_P1(1)

        for i in range(NT):
            fm = SCR[:, III_XNFM[i % 2], :].rearrange("p (c t) -> p c t", c=KC)
            t_xn = dma_sy("ds%d" % (2 + i % 2), SCR[:, III_XNFM[i % 2], :], xnsp(i),
                          waits=(free.get(("IIIxn", i % 2)),) + (BAR2 if i < 2 else ()))
            t_u_last = None
            for j in range(4):
                bank_ap = BANKS_A[j]
                pb = PHYS_A[j]
                t_mm = proj_mm(bank_ap, fm, WA, j, 2, korder2,
                               waits=(t_xn, free.get(("bank", pb)),
                                      (tok_wu_last if i == 0 and j == 0 else None)))
                t_u = ac(lambda e, j=j, bank_ap=bank_ap, i=i: e.activation(
                    utm_bufs[i % 2][:, j * 512:(j + 1) * 512], bank_ap,
                    AF.Silu), waits=(t_mm, free.get(("utm", i % 2))))
                free[("bank", pb)] = t_u
                t_u_last = t_u
            t_sp_u = dma_ac("sp%d" % (2 + i % 2), usp(i), utm_bufs[i % 2],
                            waits=(t_u_last,))
            free[("utm", i % 2)] = t_sp_u
            free[("IIIxn", i % 2)] = t_sp_u
            if i in (6, 7):
                qn = 2 + (i - 6)
                outx_toks.append(dma_gd(
                    "oxg", out_d[qn * 512:(qn + 1) * 512, :],
                    x_d[qn * 512:(qn + 1) * 512, :]))
            if i == 9:
                t_cci = dma_gd("ccd", cci_d, kvsb, waits=(t_kvev,))
                P.emit("gpsimd", lambda e: e.collective_compute(
                    "AllReduce", ALU.add,
                    replica_groups=[[0, 1, 2, 3], [4, 5, 6, 7]],
                    ins=[cci_d.opt()], outs=[cco_d.opt()]),
                    waits=(t_cci,), inc="cc")
            if i == 10:
                t_kvbf = dma_gd("ccd2", KVBF[:, :], cco_d, waits=(("cc", 1),))

            if 1 <= i <= 8:
                III_P3(2 * (i - 1))
                III_P3(2 * i - 1)
            if i <= 7:
                III_P2(2 * i)
                III_P2(2 * i + 1)
            if i <= 6:
                III_P1(2 * i + 2)
                III_P1(2 * i + 3)
            if i <= 5:
                III_wf(2 * i + 4)
                III_wf(2 * i + 5)
            if i == 8:
                t_g16d = t_g16_last[0]

        BAR3 = barrier()

        pe_fence3 = ("pe", P.counts.get("pe", 0))
        gamma_rep = scr_f32(4)
        t_grep = dma_sy("rep", gamma_rep,
                        bass.AP(g16_d.tensor, 0, [[0, 128], [1, E]]),
                        waits=(t_g16d, ("g16s", P.counts["g16s"])) + BAR3)

        IVA_XQTM, IVA_XQFM = (0, 1), (2, 3)
        xn_fs = [scr_f32(0), scr_f32(2)]
        b2_fs = [ex_f32(), ex_f32()]
        fworks = [scr_f32(6), scr_f32(8)]
        mmbank4 = [PMM[:, 0, :], PMM[:, 1, :], PKV[:, 0, :], PKV[:, 1, :]]

        stA2 = {}
        stB2 = {}
        stC2 = {}

        def IVa_A(i):
            xb = X2[:, i % 2, :]
            xn_f = xn_fs[i % 2]
            t_x = dma_ac("ds%d" % (i % 2), xb, x_d[i * 128:(i + 1) * 128, :],
                         waits=(free.get(("Ax", i % 2)),) + (BAR3 if i < 2 else ()))
            rstd, nb, t_nb, t_ss = ln_chain(xb, psum_dummy(0), t_x,
                                            free.get(("sqd", 0)))
            free[("sqd", 0)] = t_ss
            t_xnA = dv(lambda e, xb=xb, rstd=rstd, nb=nb, xn_f=xn_f:
                       e.tensor_scalar(xn_f, xb, rstd, nb, ALU.mult, ALU.add),
                       waits=(t_nb, free.get(("xnf", i % 2)))
                       + (BAR3 if i < 2 else ()))
            stA2[i] = t_xnA

        def IVa_B(i):
            t_xnA = stA2.pop(i)
            xb = X2[:, i % 2, :]
            xn_f = xn_fs[i % 2]
            b2_f = b2_fs[i % 2]
            ssn = stat(); lnr = stat(); irms = stat()
            amax = stat(); xsd = stat(); invd = stat()
            t_ssn = ac(lambda e, ssn=ssn, xn_f=xn_f: e.activation(
                psum_dummy(1), xn_f, AF.Square, accum_out=ssn),
                waits=(t_xnA, free.get(("sqd", 0))))
            free[("sqd", 0)] = t_ssn
            t_lnr = ac(lambda e, ssn=ssn, lnr=lnr: e.activation(
                lnr, ssn, AF.Ln, bias=1e-8, scale=INV_E), waits=(t_ssn,))
            t_irms = ac(lambda e, lnr=lnr, irms=irms: e.activation(
                irms, lnr, AF.Exp, scale=-0.5), waits=(t_lnr,))
            t_am = dv(lambda e, amax=amax, xn_f=xn_f: e.reduce_max(
                amax, xn_f, axis=AX.X, apply_absolute_value=True),
                waits=(t_xnA,))
            t_xs = dv(lambda e, amax=amax, irms=irms, i=i: e.tensor_scalar(
                XS16[:, i:i + 1], amax, irms, CONSTS[:, 6:7],
                ALU.mult, ALU.mult), waits=(t_am, t_irms))
            t_xr = ac(lambda e, irms=irms, xn_f=xn_f, b2_f=b2_f: e.activation(
                b2_f, xn_f, AF.Copy, scale=irms),
                waits=(t_irms, free.get(("b2f", 0)) or pe_fence3)
                + (BAR3 if i < 2 else ()))
            t_xsd = dv(lambda e, xsd=xsd, i=i: e.tensor_scalar(
                xsd, XS16[:, i:i + 1], 1e-8, None, ALU.add), waits=(t_xs,))
            t_invd = dv(lambda e, xsd=xsd, invd=invd: e.reciprocal(invd, xsd),
                        waits=(t_xsd,))
            t_zt = ac(lambda e, invd=invd, xb=xb, b2_f=b2_f: e.activation(
                xb, b2_f, AF.Copy, bias=MAGIC, scale=invd),
                waits=(t_invd, t_xr))
            free[("xnf", i % 2)] = t_xr
            t_c1 = dv(lambda e, xb=xb, b2_f=b2_f: e.tensor_scalar(
                b2_f, xb, MAGIC, -127.0, ALU.subtract, ALU.max),
                waits=(t_zt,))
            free[("Ax", i % 2)] = t_c1
            t_xq = dv(lambda e, b2_f=b2_f, i=i: e.tensor_scalar(
                wa_bf(IVA_XQTM[i % 2]), b2_f, 127.0, None, ALU.min),
                waits=(t_c1, free.get(("xqtm", i % 2)) or pe_fence3))
            free[("b2f", 0)] = t_xq
            stB2[i] = (t_xq, t_xs)

        def IVa_C1(i):
            t_xq, t_xs = stB2.pop(i)
            fmslot = IVA_XQFM[i % 2]
            t_xqfm = transpose_tile(wa_bf(IVA_XQTM[i % 2]), wa_fm(fmslot), t_xq,
                                    free.get(("xqfm", i % 2)) or pe_fence3, free,
                                    evac="ac")
            free[("xqtm", i % 2)] = t_xqfm
            t_mms = []
            for j in range(4):

                def ffn_fn(e, j=j, fmslot=fmslot):
                    last = None
                    for c in range(KC):
                        last = e.matmul(mmbank4[j], wa_fm(fmslot)[:, c, :],
                                        WB[:, c, j * 512:(j + 1) * 512],
                                        start=(c == 0), stop=(c == KC - 1))
                    return last
                t_mm = pe(ffn_fn, waits=(t_xqfm, free.get(("bank", PHYS_A[j]))))
                t_mms.append(t_mm)
            free[("xqfm", i % 2)] = t_mms[-1]
            stC2[i] = (t_mms, t_xs)

        def IVa_C2(i):
            t_mms, t_xs = stC2.pop(i)
            fw = fworks[i % 2]
            ev_last = None
            for j in range(4):
                t_f1 = dv(lambda e, j=j, i=i, fw=fw: e.scalar_tensor_tensor(
                    out=fw[:, j * 512:(j + 1) * 512], in0=mmbank4[j],
                    scalar=XS16[:, i:i + 1],
                    in1=gamma_rep[:, j * 512:(j + 1) * 512],
                    op0=ALU.mult, op1=ALU.mult),
                    waits=(t_mms[j], t_xs, free.get(("fw", i % 2)),
                           (t_grep if i == 0 else None))
                    + (BAR3 if i == 0 else ()))
                free[("bank", PHYS_A[j])] = t_f1
                ev_last = t_f1
            t_out = dma_gd("oa%d" % (i % 2), out_d[i * 128:(i + 1) * 128, :],
                           fw,
                           waits=(ev_last, ("ox", P.counts["ox"]),
                                  ("oxg", P.counts["oxg"])), accum=True)
            free[("fw", i % 2)] = t_out
            out_written.append(t_out)
            if i < 12:
                load_weight_chunk1("wta", WA, wo_d, 4 + i,
                                   waits=((pe_fence3,) if i == 0 else ()))

        out_written = []
        for k in range(NT + 3):
            if k >= 3:
                IVa_C2(k - 3)
            if 2 <= k <= NT + 1:
                IVa_C1(k - 2)
            if 1 <= k <= NT:
                IVa_B(k - 1)
            if k < NT:
                IVa_A(k)

        BAR4 = barrier()

        tok_wo_lo = load_weight_chunks("wta", WA, wo_d, 0, 4, waits=BAR4)
        pe_fence4 = ("pe", P.counts.get("pe", 0))
        gd_fence4 = ("gd", P.counts.get("gd", 0))
        lng_rep = scr_f32(6)
        lnb_rep = scr_f32(8)
        t_lgrep = dma_sy("rep", lng_rep,
                         bass.AP(lng_d.tensor, 0, [[0, 128], [1, E]]),
                         waits=BAR4)
        t_lbrep = dma_sy("rep", lnb_rep,
                         bass.AP(lnb_d.tensor, 0, [[0, 128], [1, E]]),
                         waits=BAR4)

        B_QIN, B_QFM, B_UIN, B_S, B_SFM = (0, 1), (2, 3), (4, 5), (6, 7), (8, 9)
        out_tms = [scr_f32(0), scr_f32(2)]
        t1_fs = [scr_f32(4), wb_f32(10)]
        attn_fs = [wb_f32(12), wb_f32(14)]
        psum_dummy_b = SQD[:, :]

        stF = {}
        stG = {}
        stH = {}

        def IVb_F(i):
            t_qi = dma_sy("ds%d" % (2 + i % 2), wb_bf(B_QIN[i % 2]), qsp(i),
                          waits=(free.get(("Bq", i % 2)),) + (BAR4 if i < 2 else ()))
            spall4 = ((("sp2", P.counts["sp2"]), ("sp3", P.counts["sp3"]))
                      if i < 2 else ())
            t_ui = dma_sy("ds%d" % (4 + i % 2), wb_bf(B_UIN[i % 2]), usp(i),
                          waits=(free.get(("Bu", i % 2)),) + spall4
                          + (BAR4 if i < 2 else ()))
            fmslot = B_QFM[i % 2]
            t_qfm = transpose_tile(wb_bf(B_QIN[i % 2]), wb_fm(fmslot), t_qi,
                                   free.get(("Bqfm", i % 2)) or pe_fence4, free,
                                   evac="ac")
            free[("Bq", i % 2)] = t_qfm

            out_tm = out_tms[i % 2]

            def oe_fn(e, fmslot=fmslot, half=0):
                last = None
                qfm = wb_fm(fmslot)
                for h in range(half * 8, half * 8 + 8):
                    last = e.matmul(
                        PKV[:, (h // 4) % 2, (h % 4) * 128:(h % 4 + 1) * 128],
                        qfm[:, h, :], KVBF[:, h * 128:(h + 1) * 128],
                        start=True, stop=True)
                return last
            t_oe1 = pe(lambda e, fmslot=fmslot: oe_fn(e, fmslot, 0),
                       waits=(t_qfm, free.get(("pkv", 0)),
                              (t_kvbf if i == 0 else None))
                       + (BAR4 if i == 0 else ()))
            t_oev1 = ac(lambda e, out_tm=out_tm: [
                e.activation(out_tm[:, b * 512:(b + 1) * 512], PKV[:, b, :],
                             AF.Copy)
                for b in range(2)][-1],
                waits=(t_oe1, free.get(("outtm", i % 2)))
                + (BAR4 if i < 2 else ()))
            t_oe2 = pe(lambda e, fmslot=fmslot: oe_fn(e, fmslot, 1),
                       waits=(t_oev1,))
            free[("Bqfm", i % 2)] = t_oe2
            t_oev = ac(lambda e, out_tm=out_tm: [
                e.activation(out_tm[:, 1024 + b * 512:1024 + (b + 1) * 512],
                             PKV[:, b, :], AF.Copy)
                for b in range(2)][-1],
                waits=(t_oe2,))
            free[("pkv", 0)] = t_oev
            stF[i] = (t_oev, t_ui)

        def IVb_G(i):
            t_oev, t_ui = stF.pop(i)
            out_tm = out_tms[i % 2]
            rstd, nb, t_nb, t_ss = ln_chain(out_tm, psum_dummy_b, t_oev,
                                            free.get(("sqd", 0)))
            free[("sqd", 0)] = t_ss
            t1_f = t1_fs[i % 2]
            t_t1 = dv(lambda e, rstd=rstd, nb=nb, out_tm=out_tm, t1_f=t1_f:
                      e.tensor_scalar(t1_f, out_tm, rstd, nb, ALU.mult, ALU.add),
                      waits=(t_nb, free.get(("t1f", i % 2)))
                      + (BAR4 if i < 2 else ()) + ((pe_fence4,) if i < 2 else ()))
            t_p1 = dv(lambda e, out_tm=out_tm, t1_f=t1_f: e.tensor_mul(
                out_tm, t1_f, lng_rep),
                waits=(t_t1, (t_lbrep if i == 0 else None)))
            free[("outtm", i % 2)] = t_p1
            t_p2 = dv(lambda e, out_tm=out_tm, t1_f=t1_f: e.tensor_add(
                t1_f, out_tm, lnb_rep),
                waits=(t_p1, (t_lbrep if i == 0 else None)))
            pass
            sslot = B_S[i % 2]
            t_s = dv(lambda e, sslot=sslot, t1_f=t1_f, i=i: e.tensor_mul(
                wb_bf(sslot), t1_f, wb_bf(B_UIN[i % 2])),
                waits=(t_p2, t_ui, free.get(("Bs", i % 2)) or pe_fence4))
            free[("Bu", i % 2)] = t_s
            free[("t1f", i % 2)] = t_s
            stG[i] = (t_s, sslot)

        def IVb_H1(i):
            t_s, sslot = stG.pop(i)
            sfm = B_SFM[i % 2]
            t_sfm = transpose_tile(wb_bf(sslot), wb_fm(sfm), t_s,
                                   free.get(("Bsfm", i % 2)) or pe_fence4, free,
                                   evac="ac")
            free[("Bs", i % 2)] = t_sfm
            attn_f = attn_fs[i % 2]
            WO_BANKS = [PMM[:, 0, :], PMM[:, 1, :], PKV[:, 2, :], PKV[:, 3, :]]
            WO_PHYS = (0, 1, 6, 7)
            t_mms = []
            for j in range(4):
                t_mm = proj_mm(WO_BANKS[j], wb_fm(sfm), WA, j, 3, range(KC),
                               waits=(t_sfm, free.get(("bank", WO_PHYS[j])),
                                      (tok_wo_lo if i == 0 and j == 0 else None)))
                t_mms.append(t_mm)
            free[("Bsfm", i % 2)] = t_mms[-1]
            stH[i] = t_mms

        def IVb_H2(i):
            t_mms = stH.pop(i)
            attn_f = attn_fs[i % 2]
            WO_BANKS = [PMM[:, 0, :], PMM[:, 1, :], PKV[:, 2, :], PKV[:, 3, :]]
            WO_PHYS = (0, 1, 6, 7)
            ev_last = None
            for j in range(4):
                t_ev = dv(lambda e, j=j, attn_f=attn_f: e.tensor_copy(
                    attn_f[:, j * 512:(j + 1) * 512], WO_BANKS[j]),
                    waits=(t_mms[j], free.get(("attnf", i % 2)))
                    + ((pe_fence4,) if i < 2 and j == 0 else ()))
                free[("bank", WO_PHYS[j])] = t_ev
                ev_last = t_ev
            t_acc = dma_gd("ob%d" % (i % 2), out_d[i * 128:(i + 1) * 128, :],
                           attn_f,
                           waits=(ev_last, out_written[i]), accum=True)
            free[("attnf", i % 2)] = t_acc

        for k in range(NT + 3):
            if k >= 3:
                IVb_H2(k - 3)
            if 2 <= k <= NT + 1:
                IVb_H1(k - 2)
            if 1 <= k <= NT:
                IVb_G(k - 1)
            if k < NT:
                IVb_F(k)

        finals = [(s, P.counts.get(s, 0)) for s in ("oa0", "oa1", "ob0", "ob1")]

        with nc.Block() as block:
            def mk(eng_name):
                def body(eng):
                    satisfied = {}
                    for fn, waits, inc, amt in P.steps[eng_name]:
                        for (s, v) in waits:
                            if satisfied.get(s, 0) >= v:
                                continue
                            eng.wait_ge(sems[s], v)
                            satisfied[s] = v
                        inst = fn(eng)
                        if inc is not None:
                            inst.then_inc(sems[inc], amt)
                    if eng_name == "gpsimd":
                        for s, v in finals:
                            eng.wait_ge(sems[s], v)
                return body

            block.sync(mk("sync"))
            block.gpsimd(mk("gpsimd"))
            block.tensor(mk("tensor"))
            block.vector(mk("vector"))
            block.scalar(mk("scalar"))

    return nc


_NC_CACHE = None


def kernel(L0_attn_x, L0_attn_Wqk, L0_attn_bqk, L0_attn_Wv, L0_attn_bv,
           L0_attn_Wu, L0_attn_bu, L0_attn_Wo, L0_attn_bo,
           L0_attn_ln_g, L0_attn_ln_b, L0_ffn_weight, L0_ffn_bias):
    global _NC_CACHE
    x = np.ascontiguousarray(np.asarray(L0_attn_x, dtype=np.float32))
    B, N, E_ = x.shape
    xf = x.reshape(B * N, E_)

    def arr(a):
        return np.ascontiguousarray(np.asarray(a, dtype=np.float32))

    def row(a):
        return arr(a).reshape(1, E_)

    consts = np.zeros((1, 16), np.float32)
    for i, v in enumerate(REG_CONSTS):
        consts[0, i] = v

    common = {
        "wqk": arr(L0_attn_Wqk), "wv": arr(L0_attn_Wv), "wu": arr(L0_attn_Wu),
        "wo": arr(L0_attn_Wo), "wf": arr(L0_ffn_weight),
        "bqk": row(L0_attn_bqk), "bv": row(L0_attn_bv), "bu": row(L0_attn_bu),
        "bo": row(L0_attn_bo), "fbias": row(L0_ffn_bias),
        "lng": row(L0_attn_ln_g), "lnb": row(L0_attn_ln_b),
        "cdb": consts,
        "sel": SEL_NP, "negm": NEGM_NP, "ident": IDENT_NP,
    }
    in_maps = []
    for c in range(8):
        m = dict(common)
        m["x"] = np.ascontiguousarray(xf[c * TOK:(c + 1) * TOK, :])
        in_maps.append(m)

    if _NC_CACHE is None:
        _NC_CACHE = build_nc()
    res = run_bass_kernel_spmd(_NC_CACHE, in_maps, core_ids=list(range(8)))
    out = np.concatenate([res.results[c]["out"] for c in range(8)], axis=0)
    return out.reshape(B, N, E_).astype(np.float32)


if __name__ == "__main__":
    nc = build_nc()
    print("built ok; instructions:",
          sum(1 for _ in nc.all_instructions()))


# version 23
# speedup vs baseline: 1.0528x; 1.0221x over previous
from contextlib import ExitStack

import ml_dtypes
import numpy as np

import concourse.bass as bass
import concourse.mybir as mybir
from concourse.bass_utils import run_bass_kernel_spmd

F32 = mybir.dt.float32
BF16 = mybir.dt.bfloat16
AF = mybir.ActivationFunctionType
ALU = mybir.AluOpType
AX = mybir.AxisListType

MAGIC = float(1.5 * 2 ** 23)
E = 2048
KC = 16
TOK = 2048
NT = 16
HEADS = 16
INV_E = 1.0 / 2048.0
REG_CONSTS = [1e-5, 1e-8, -1.0, MAGIC, -0.5, INV_E, 1.0 / 127.0]

ENGINES = ("sync", "gpsimd", "tensor", "vector", "scalar")

SEL_NP = np.zeros((10, 4 * 128), dtype=ml_dtypes.bfloat16)
for _b in range(4):
    SEL_NP[2 * _b:2 * _b + 2, _b * 128:(_b + 1) * 128] = 1
SEL_NP[8:10, 3 * 128:4 * 128] = 1
NEGM_NP = np.zeros((10, 1), np.float32)
NEGM_NP[1::2, 0] = -1.0
IDENT_NP = np.eye(128, dtype=ml_dtypes.bfloat16)


class Plan:
    def __init__(self):
        self.steps = {e: [] for e in ENGINES}
        self.counts = {}
        self.last = {}
        self.eng_dma = {}

    def emit(self, eng, fn, waits=(), inc=None, amt=1):
        tok = None
        if inc is not None:
            self.counts[inc] = self.counts.get(inc, 0) + amt
            tok = (inc, self.counts[inc])
            self.last[eng] = tok
            if amt == 16:
                self.eng_dma.setdefault(eng, {})[inc] = self.counts[inc]
        self.steps[eng].append(
            (fn, tuple(w for w in waits if w is not None), inc, amt))
        return tok


def build_nc():
    nc = bass.Bass("TRN2", target_bir_lowering=False)

    x_d = nc.dram_tensor("x", [TOK, E], F32, kind="ExternalInput").ap()
    wqk_d = nc.dram_tensor("wqk", [E, E], F32, kind="ExternalInput").ap()
    wv_d = nc.dram_tensor("wv", [E, E], F32, kind="ExternalInput").ap()
    wu_d = nc.dram_tensor("wu", [E, E], F32, kind="ExternalInput").ap()
    wo_d = nc.dram_tensor("wo", [E, E], F32, kind="ExternalInput").ap()
    wf_d = nc.dram_tensor("wf", [E, E], F32, kind="ExternalInput").ap()
    bqk_d = nc.dram_tensor("bqk", [1, E], F32, kind="ExternalInput").ap()
    bv_d = nc.dram_tensor("bv", [1, E], F32, kind="ExternalInput").ap()
    bu_d = nc.dram_tensor("bu", [1, E], F32, kind="ExternalInput").ap()
    bo_d = nc.dram_tensor("bo", [1, E], F32, kind="ExternalInput").ap()
    fbias_d = nc.dram_tensor("fbias", [1, E], F32, kind="ExternalInput").ap()
    lng_d = nc.dram_tensor("lng", [1, E], F32, kind="ExternalInput").ap()
    lnb_d = nc.dram_tensor("lnb", [1, E], F32, kind="ExternalInput").ap()
    cdb_d = nc.dram_tensor("cdb", [1, 16], F32, kind="ExternalInput").ap()
    sel_d = nc.dram_tensor("sel", [10, 4 * 128], BF16, kind="ExternalInput").ap()
    negm_d = nc.dram_tensor("negm", [10, 1], F32, kind="ExternalInput").ap()
    ident_d = nc.dram_tensor("ident", [128, 128], BF16, kind="ExternalInput").ap()
    out_d = nc.dram_tensor("out", [TOK, E], F32, kind="ExternalOutput").ap()
    xnsp_d = nc.dram_tensor("xnsp", [128, NT * KC * 128], BF16).ap()
    qsp_d = nc.dram_tensor("qsp", [128, NT * E], BF16).ap()
    usp_d = nc.dram_tensor("usp", [128, NT * E], BF16).ap()
    g16_d = nc.dram_tensor("g16d", [1, E], F32).ap()
    cci_d = nc.dram_tensor("cci", [128, HEADS * 128], F32).ap()
    cco_d = nc.dram_tensor("cco", [128, HEADS * 128], F32).ap()

    def xnsp(i):
        return xnsp_d[:, i * KC * 128:(i + 1) * KC * 128]

    def qsp(i):
        return qsp_d[:, i * E:(i + 1) * E]

    def usp(i):
        return usp_d[:, i * E:(i + 1) * E]

    with ExitStack() as ctx:
        ec = ctx.enter_context
        WA = ec(nc.sbuf_tensor("WA", [128, KC, E], BF16))
        WB = ec(nc.sbuf_tensor("WB", [128, KC, E], BF16))
        SCR = ec(nc.sbuf_tensor("SCR", [128, 10, E], BF16))
        X2 = ec(nc.sbuf_tensor("X2", [128, 2, E], F32))
        EXTRA = ec(nc.sbuf_tensor("EXTRA", [128, 2, E], BF16))
        BIASBF = ec(nc.sbuf_tensor("BIASBF", [10, E], BF16))
        SEL = ec(nc.sbuf_tensor("SEL", [10, 4 * 128], BF16))
        NEGM = ec(nc.sbuf_tensor("NEGM", [10, 1], F32))
        KVBF = ec(nc.sbuf_tensor("KVBF", [128, HEADS * 128], BF16))
        IDENT = ec(nc.sbuf_tensor("IDENT", [128, 128], BF16))
        STATS = ec(nc.sbuf_tensor("STATS", [128, 576], F32))
        G16 = ec(nc.sbuf_tensor("G16", [128, KC], F32))
        XS16 = ec(nc.sbuf_tensor("XS16", [128, NT], F32))
        CONSTS = ec(nc.sbuf_tensor("CONSTS", [128, 16], F32))
        SQD = ec(nc.sbuf_tensor("SQD", [128, E], BF16))
        PMM = ec(nc.psum_tensor("PMM", [128, 2, 512], F32))
        PTR = ec(nc.psum_tensor("PTR", [128, 2, 1024], BF16))
        PKV = ec(nc.psum_tensor("PKV", [128, 4, 512], F32))

        sems = {}
        for s in ("pe", "dv", "ac", "cc", "init", "rep", "g16s",
                  "wta", "wtA1", "wtA2", "wtA3", "wtA4", "wtA5", "wtb", "wtc", "ox", "oxg", "ccd", "ccd2", "wf0", "wf1", "wf2", "wf3",
                  "bsy", "bgd", "bac",
                  "ds0", "ds1", "ds2", "ds3", "ds4", "ds5",
                  "sp0", "sp1", "sp2", "sp3",
                  "oa0", "oa1", "ob0", "ob1"):
            sems[s] = ec(nc.semaphore(s))

        for i, v in enumerate(REG_CONSTS):
            nc.const_aps.register(F32, v, CONSTS[:, i:i + 1])

        P = Plan()

        def dma_sy(sem, out, in_, waits=()):
            return P.emit("sync", lambda e, o=out, i=in_: e.dma_start(out=o, in_=i),
                          waits=waits, inc=sem, amt=16)

        def dma_gd(sem, out, in_, waits=(), accum=False):
            if accum:
                return P.emit(
                    "gpsimd",
                    lambda e, o=out, i=in_: e.dma_start(out=o, in_=i,
                                                        accum_op=ALU.add),
                    waits=waits, inc=sem, amt=16)
            return P.emit("gpsimd",
                          lambda e, o=out, i=in_: e.dma_start(out=o, in_=i),
                          waits=waits, inc=sem, amt=16)

        def pe(fn, waits=()):
            return P.emit("tensor", fn, waits=waits, inc="pe")

        def dv(fn, waits=()):
            return P.emit("vector", fn, waits=waits, inc="dv")

        def ac(fn, waits=()):
            return P.emit("scalar", fn, waits=waits, inc="ac")

        def barrier():
            toks = []
            for eng, bsem in (("sync", "bsy"), ("gpsimd", "bgd"),
                              ("scalar", "bac")):
                dws = tuple((s, v) for s, v in
                            P.eng_dma.get(eng, {}).items())
                toks.append(P.emit(eng, lambda e: e.nop(),
                                   waits=dws, inc=bsem))
            for eng in ("tensor", "vector"):
                if P.last.get(eng):
                    toks.append(P.last[eng])
            if P.last.get("scalar") and P.last["scalar"][0] == "ac":
                toks.append(P.last["scalar"])
            return tuple(toks)

        def dma_ac(sem, out, in_, waits=()):
            return P.emit("scalar", lambda e, o=out, i=in_: e.dma_start(out=o, in_=i),
                          waits=waits, inc=sem, amt=16)

        def scr_bf(i):
            return SCR[:, i, :]

        def scr_f32(i):
            return SCR[:, i:i + 2, :].bitcast(F32).rearrange("p a b -> p (a b)")

        def wa_bf(i):
            return WA[:, i, :]

        def wb_bf(i):
            return WB[:, i, :]

        def wa_fm(i):
            return WA[:, i, :].rearrange("p (c t) -> p c t", c=KC)

        def wb_fm(i):
            return WB[:, i, :].rearrange("p (c t) -> p c t", c=KC)

        def wa_f32(i):
            return WA[:, i:i + 2, :].bitcast(F32).rearrange("p a b -> p (a b)")

        def wb_f32(i):
            return WB[:, i:i + 2, :].bitcast(F32).rearrange("p a b -> p (a b)")

        def ex_bf(i):
            return EXTRA[:, i, :]

        def ex_f32():
            return EXTRA[:, 0:2, :].bitcast(F32).rearrange("p a b -> p (a b)")

        def psum_dummy(i):
            return SQD[:, :]

        def load_weight_chunks(sem, wdst, wsrc, c0, c1, waits=()):
            src = wsrc.rearrange("(c p) m -> p c m", p=128)[:, c0:c1, :]
            return dma_gd(sem, wdst[:, c0:c1, :], src, waits=waits)

        def load_weight_chunk1(sem, wdst, wsrc, c, waits=()):
            src = wsrc.rearrange("(c p) m -> p c m", p=128)[:, c:c + 1, :]
            return dma_gd(sem, wdst[:, c:c + 1, :], src, waits=waits)

        st_col = [0]

        def stat(n=1):
            c = st_col[0]
            st_col[0] += n
            assert st_col[0] <= 576
            return STATS[:, c:c + n]

        def transpose_tile(src_bf, dst_fm, src_tok, dst_free_tok, free,
                           evac="dv"):
            last = None
            emitf = dv if evac == "dv" else ac
            for h in range(2):
                t_tr = pe(lambda e, h=h, s=src_bf: [
                    e.transpose(PTR[:, h, c * 128:(c + 1) * 128],
                                s[:, (h * 8 + c) * 128:(h * 8 + c + 1) * 128],
                                IDENT[:])
                    for c in range(8)][-1],
                    waits=(src_tok, free.get(("ptr", h))))
                if evac == "dv":
                    t_ev = dv(lambda e, h=h, d=dst_fm: e.tensor_copy(
                        d[:, h * 8:(h + 1) * 8, :].rearrange("p c t -> p (c t)"),
                        PTR[:, h, :]),
                        waits=(t_tr, dst_free_tok))
                else:
                    t_ev = ac(lambda e, h=h, d=dst_fm: e.activation(
                        d[:, h * 8:(h + 1) * 8, :].rearrange("p c t -> p (c t)"),
                        PTR[:, h, :], AF.Copy),
                        waits=(t_tr, dst_free_tok))
                free[("ptr", h)] = t_ev
                last = t_ev
            return last

        def ln_chain(xb, sq_dst, t_x, sq_free, extra=()):
            s1 = stat(); ss = stat(); a = stat(); vp = stat()
            lnv = stat(); rstd = stat(); nb = stat()
            t_s1 = dv(lambda e, xb=xb, s1=s1: e.reduce_sum(s1, xb, axis=AX.X),
                      waits=(t_x,))
            t_ss = ac(lambda e, xb=xb, ss=ss, d=sq_dst: e.activation(
                d, xb, AF.Square, accum_out=ss), waits=(t_x, sq_free))
            t_a = dv(lambda e, s1=s1, a=a: e.scalar_tensor_tensor(
                out=a, in0=s1, scalar=-INV_E, in1=s1,
                op0=ALU.mult, op1=ALU.mult), waits=(t_s1,))
            t_vp = dv(lambda e, a=a, ss=ss, vp=vp: e.tensor_add(vp, a, ss),
                      waits=(t_a, t_ss))
            t_lnv = ac(lambda e, vp=vp, lnv=lnv: e.activation(
                lnv, vp, AF.Ln, bias=1e-5, scale=INV_E),
                waits=(t_vp,) + tuple(extra))
            t_rstd = ac(lambda e, lnv=lnv, rstd=rstd: e.activation(
                rstd, lnv, AF.Exp, scale=-0.5), waits=(t_lnv,))
            t_nb = dv(lambda e, s1=s1, rstd=rstd, nb=nb: e.scalar_tensor_tensor(
                out=nb, in0=s1, scalar=-INV_E, in1=rstd,
                op0=ALU.mult, op1=ALU.mult), waits=(t_rstd,))
            return rstd, nb, t_nb, t_ss

        free = {}

        tok_consts = dma_sy("init", CONSTS[:, :],
                            bass.AP(cdb_d.tensor, 0, [[0, 128], [1, 16]]))
        tok_ident = dma_sy("init", IDENT[:, :], ident_d)
        tok_sel = dma_sy("init", SEL[:, :], sel_d)
        tok_negm = dma_sy("init", NEGM[:, :], negm_d)

        bstage = SCR[0:10, 8:10, :].bitcast(F32).rearrange("p a b -> p (a b)")
        for b, bd in enumerate((bqk_d, bv_d, bu_d, bo_d, fbias_d)):
            dma_sy("init",
                   SCR[2 * b:2 * b + 2, 8:10, :].bitcast(F32)
                   .rearrange("p a b -> p (a b)"),
                   bass.AP(bd.tensor, 0, [[0, 2], [1, E]]))
        init_all = ("init", P.counts["init"])
        bias_toks = {}

        def emit_bias_build():
            t_hi = dv(lambda e: e.tensor_copy(BIASBF[:, :], bstage),
                      waits=(init_all,))
            bias_toks["lo"] = dv(lambda e: e.scalar_tensor_tensor(
                out=BIASBF[:, :], in0=BIASBF[:, :], scalar=NEGM[:, :],
                in1=bstage, op0=ALU.mult, op1=ALU.add),
                waits=(t_hi,))
        tok_wa_pairs = [load_weight_chunks("wta", WA, wqk_d, 0, 2),
                        load_weight_chunks("wtA1", WA, wqk_d, 2, 4),
                        load_weight_chunks("wtA2", WA, wqk_d, 4, 6),
                        load_weight_chunks("wtA3", WA, wqk_d, 6, 8),
                        load_weight_chunks("wtA4", WA, wqk_d, 12, 14),
                        load_weight_chunks("wtA5", WA, wqk_d, 14, 16)]
        wl_toks = [dma_sy("ds4", scr_f32(4), wqk_d[8 * 128:9 * 128, :]),
                   dma_sy("ds5", scr_f32(6), wqk_d[9 * 128:10 * 128, :])]
        wa_cast_toks = []

        def emit_wa_cast(c):
            stg8 = scr_f32(4 if c % 2 == 0 else 6)
            wa_cast_toks.append(dv(
                lambda e, s=stg8, c=c: e.tensor_copy(WA[:, c, :], s),
                waits=(wl_toks[c - 8],)))

        PTRF = [PTR[:, h, :].bitcast(F32) for h in range(2)]
        BANKS_A = [PMM[:, 0, :], PMM[:, 1, :], PKV[:, 0, :], PKV[:, 1, :]]
        BANKS_B = [PMM[:, 0, :], PMM[:, 1, :], PTRF[0], PTRF[1]]
        PHYS_A = (0, 1, 4, 5)
        PHYS_B = (0, 1, 2, 3)

        def proj_mm(bank_ap, fm, W, j, brow, ks, waits):
            def fn(e, bank_ap=bank_ap, fm=fm, W=W, j=j, brow=brow, ks=tuple(ks)):
                for idx, c in enumerate(ks):
                    e.matmul(bank_ap, fm[:, c, :],
                             W[:, c, j * 512:(j + 1) * 512],
                             start=(idx == 0), stop=False)
                return e.matmul(bank_ap,
                                SEL[:, brow * 128:(brow + 1) * 128],
                                BIASBF[:, j * 512:(j + 1) * 512],
                                start=False, stop=True)
            return pe(fn, waits=waits)

        free[("ptr", 0)] = init_all
        free[("ptr", 1)] = init_all
        I_XNFM, I_QTM = (1, 2), (3, 4)
        elu_scr = [scr_f32(0), scr_f32(2)]
        outx_toks = []
        wv_pre = [0, 5] + list(range(6, KC))
        korder2 = wv_pre[2:] + wv_pre[:2] + [1, 2, 3, 4]
        korder2 = list(korder2)

        stA = {}
        stB = {}

        def I_A(i):
            xb = X2[:, i % 2, :]
            t_x = dma_sy("ds%d" % (i % 2), xb, x_d[i * 128:(i + 1) * 128, :],
                         waits=(free.get(("x", i % 2)),))
            rstd, nb, t_nb, t_ss = ln_chain(xb, psum_dummy(0), t_x,
                                            free.get(("sqd", 0)),
                                            extra=((init_all,) if i < 2 else ()))
            free[("sqd", 0)] = t_ss
            t_xnbf = dv(lambda e, xb=xb, rstd=rstd, nb=nb, i=i: e.tensor_scalar(
                ex_bf(i % 2), xb, rstd, nb, ALU.mult, ALU.add),
                waits=(t_nb, free.get(("xnbf", i % 2))))
            free[("x", i % 2)] = t_xnbf
            stA[i] = t_xnbf

        def I_B(i):
            t_xnbf = stA.pop(i)
            fmslot = I_XNFM[i % 2]
            t_xnfm = transpose_tile(ex_bf(i % 2), wb_fm(fmslot), t_xnbf,
                                    free.get(("xnfm", i % 2)), free)
            free[("xnbf", i % 2)] = t_xnfm
            t_sp_xn = dma_gd("sp%d" % (i % 2), xnsp(i), wb_bf(fmslot),
                             waits=(t_xnfm,))
            stB[i] = (t_xnfm, t_sp_xn)

        def I_C(i):
            t_xnfm, t_sp_xn = stB.pop(i)
            fmslot = I_XNFM[i % 2]
            qslot = I_QTM[i % 2]
            t_q_last = None
            for j in range(4):
                bank_ap = BANKS_A[j]
                pb = PHYS_A[j]
                if i == 0:
                    for ci, c in enumerate(range(KC)):
                        wv = (tok_wa_pairs[c // 2] if c < 8
                              else (wa_cast_toks[c - 8] if c < 12
                                    else tok_wa_pairs[4 + (c - 12) // 2]))
                        pe(lambda e, bank_ap=bank_ap, fmslot=fmslot, c=c, ci=ci, j=j:
                           e.matmul(bank_ap, wb_fm(fmslot)[:, c, :],
                                    WA[:, c, j * 512:(j + 1) * 512],
                                    start=(ci == 0), stop=False),
                           waits=(t_xnfm, free.get(("bank", pb)), wv))
                    t_mm = pe(lambda e, bank_ap=bank_ap, j=j: e.matmul(
                        bank_ap, SEL[:, 0:128],
                        BIASBF[:, j * 512:(j + 1) * 512],
                        start=False, stop=True),
                        waits=((bias_toks["lo"],) if j == 0 else ()))
                else:
                    t_mm = proj_mm(bank_ap, wb_fm(fmslot), WA, j, 0, range(KC),
                                   waits=(t_xnfm, free.get(("bank", pb)),
                                          (tok_wa if i == 1 else None))
                                   + (tuple(tok_wa_pairs) if i == 1 else ()))
                eslot = j % 2
                ebuf = elu_scr[eslot][:, 0:512]
                rbuf = elu_scr[eslot][:, 512:1024]
                sbuf = elu_scr[eslot][:, 1024:1536]
                t_e = ac(lambda e, bank_ap=bank_ap, eb=ebuf: e.activation(
                    eb, bank_ap, AF.Exp),
                    waits=(t_mm, free.get(("elu", eslot))))
                t_r = ac(lambda e, bank_ap=bank_ap, rb=rbuf: e.activation(
                    rb, bank_ap, AF.Relu), waits=(t_mm,))
                t_s = dv(lambda e, eb=ebuf, rb=rbuf, sb=sbuf: e.scalar_tensor_tensor(
                    out=sb, in0=eb, scalar=1.0, in1=rb, op0=ALU.min, op1=ALU.add),
                    waits=(t_e, t_r, free.get(("elus", eslot))))
                free[("bank", pb)] = t_r
                t_q = ac(lambda e, sb=sbuf, j=j, qslot=qslot: e.activation(
                    wb_bf(qslot)[:, j * 512:(j + 1) * 512], sb, AF.Copy, bias=-1.0),
                    waits=(t_s, free.get(("qtm", i % 2))))
                free[("elu", eslot)] = t_q
                free[("elus", eslot)] = t_q
                t_q_last = t_q
            t_sp_q = dma_gd("sp%d" % (i % 2), qsp(i), wb_bf(qslot),
                            waits=(t_q_last,))
            free[("xnfm", i % 2)] = t_sp_q
            free[("qtm", i % 2)] = t_sp_q
            if i < len(wv_pre):
                tok_wv_hi = load_weight_chunk1("wtb", WB, wv_d, wv_pre[i])
            if i == 14:
                for qn in (0, 1):
                    outx_toks.append(dma_ac(
                        "ox", out_d[qn * 512:(qn + 1) * 512, :],
                        x_d[qn * 512:(qn + 1) * 512, :]))

        tok_wa = None
        for k in range(NT + 2):
            if 1 <= k <= NT:
                I_B(k - 1)
            if k < NT:
                I_A(k)
            if k == 0:
                emit_wa_cast(8)
            if k == 1:
                emit_bias_build()
                emit_wa_cast(9)
                wl_toks.append(dma_sy("ds4", scr_f32(4),
                                      wqk_d[10 * 128:11 * 128, :],
                                      waits=(wa_cast_toks[0],)))
            if k == 2:
                emit_wa_cast(10)
                wl_toks.append(dma_sy("ds5", scr_f32(6),
                                      wqk_d[11 * 128:12 * 128, :],
                                      waits=(wa_cast_toks[1],)))
                emit_wa_cast(11)
                tok_wa = wa_cast_toks[-1]
            if k >= 2:
                I_C(k - 2)

        BAR1 = barrier()

        tok_wv_lo = load_weight_chunks("wtc", WB, wv_d, 1, 5, waits=BAR1)
        pe_fence1 = ("pe", P.counts.get("pe", 0))

        II_XNFM, II_QIN, II_VTM = (0, 1), (2, 3), 4
        kv_last = None
        tok_wu_last = None
        for i in range(NT):
            fm = SCR[:, II_XNFM[i % 2], :].rearrange("p (c t) -> p c t", c=KC)
            spall = ((("sp0", P.counts["sp0"]), ("sp1", P.counts["sp1"]))
                     if i < 2 else ())
            t_xn = dma_sy("ds%d" % (2 + i % 2), SCR[:, II_XNFM[i % 2], :], xnsp(i),
                          waits=(free.get(("IIxn", i % 2)),) + spall
                          + (BAR1 if i < 2 else ()))
            t_qi = dma_sy("ds%d" % (4 + i % 2), scr_bf(II_QIN[i % 2]), qsp(i),
                          waits=(free.get(("IIq", i % 2)),) + spall
                          + (BAR1 if i < 2 else ()))
            t_v_last = None
            for j in range(4):
                bank_ap = BANKS_B[j]
                pb = PHYS_B[j]
                if i == 0:
                    for ci, c in enumerate(korder2):
                        wv = (tok_wv_lo if c in (1, 2, 3, 4)
                              else ("wtb", 16 * 12))
                        pe(lambda e, bank_ap=bank_ap, fm=fm, c=c, ci=ci, j=j:
                           e.matmul(bank_ap, fm[:, c, :],
                                    WB[:, c, j * 512:(j + 1) * 512],
                                    start=(ci == 0), stop=False),
                           waits=(t_xn, free.get(("bank", pb)), wv))
                    t_mm = pe(lambda e, bank_ap=bank_ap, j=j: e.matmul(
                        bank_ap, SEL[:, 128:256],
                        BIASBF[:, j * 512:(j + 1) * 512],
                        start=False, stop=True), waits=())
                else:
                    t_mm = proj_mm(bank_ap, fm, WB, j, 1, korder2,
                                   waits=(t_xn, free.get(("bank", pb))))
                t_v = ac(lambda e, j=j, bank_ap=bank_ap: e.activation(
                    scr_bf(II_VTM)[:, j * 512:(j + 1) * 512], bank_ap,
                    AF.Silu), waits=(t_mm, free.get(("vtm", 0))))
                free[("bank", pb)] = t_v
                t_v_last = t_v

            def kv_fn(e, i=i, qs=II_QIN[i % 2]):
                last = None
                for h in range(HEADS):
                    last = e.matmul(
                        PKV[:, h // 4, (h % 4) * 128:(h % 4 + 1) * 128],
                        scr_bf(qs)[:, h * 128:(h + 1) * 128],
                        scr_bf(II_VTM)[:, h * 128:(h + 1) * 128],
                        start=(i == 0 and h % 4 == 0),
                        stop=(i == NT - 1 and h % 4 == 3))
                return last
            t_kv = pe(kv_fn, waits=(t_qi, t_v_last))
            kv_last = t_kv
            free[("vtm", 0)] = t_kv
            free[("IIxn", i % 2)] = t_kv
            free[("IIq", i % 2)] = t_kv
            if i < 8:
                for c in (korder2[2 * i], korder2[2 * i + 1]):
                    tok_wu_last = load_weight_chunk1(
                        "wta", WA, wu_d, c,
                        waits=((pe_fence1,) if i == 0 else ()))

        wf_toks = {}
        wf_toks[0] = dma_gd("wf0", X2[:, 1, :], wf_d[0:128, :],
                            waits=(free.get(("x", 1)),))
        wf_toks[1] = dma_gd("wf1", scr_f32(8), wf_d[128:256, :],
                            waits=(bias_toks["lo"],))
        wf_toks[2] = dma_gd("wf2", scr_f32(4), wf_d[256:384, :],
                            waits=(kv_last,))
        wf_toks[3] = dma_gd("wf3", scr_f32(6), wf_d[384:512, :],
                            waits=(tok_wa,))
        kvsb = X2[:, 0, :]
        t_kvev = dv(lambda e: [e.tensor_copy(kvsb[:, b * 512:(b + 1) * 512],
                                             PKV[:, b, :]) for b in range(4)][-1],
                    waits=(kv_last,))

        BAR2 = barrier()

        III_XNFM = (0, 1)
        utm_bufs = [scr_bf(2), ex_bf(1)]
        wtnk = [scr_bf(3), ex_bf(0)]
        stage_f = [X2[:, 1, :], scr_f32(8), scr_f32(4), scr_f32(6)]
        MAGICR = CONSTS[:, 3:4]
        NEG1R = CONSTS[:, 2:3]

        t_g16d = None
        t_cci = None
        t_kvbf = None
        stP1 = {}
        stP2 = {}

        def III_wf(c):
            wf_toks[c] = dma_gd("wf%d" % (c % 4), stage_f[c % 4],
                                wf_d[c * 128:(c + 1) * 128, :],
                                waits=(free.get(("stage", c % 4)),))

        def III_P1(c):
            stg = stage_f[c % 4]
            t_wf = wf_toks.pop(c)
            red = stat(); ge = stat(); ginv = stat()
            t_red = dv(lambda e, stg=stg, red=red: e.reduce_sum(
                red, stg, axis=AX.X, apply_absolute_value=True), waits=(t_wf,))
            t_g = dv(lambda e, red=red, c=c: e.tensor_scalar(
                G16[:, c:c + 1], red, INV_E, None, ALU.mult), waits=(t_red,))
            t_ge = dv(lambda e, ge=ge, c=c: e.tensor_scalar(
                ge, G16[:, c:c + 1], 1e-8, None, ALU.add), waits=(t_g,))
            t_gi = dv(lambda e, ge=ge, ginv=ginv: e.reciprocal(ginv, ge),
                      waits=(t_ge,))
            stP1[c] = (t_gi, ginv, t_g)

        def III_P2(c):
            t_gi, ginv, t_g = stP1.pop(c)
            stg = stage_f[c % 4]
            t_d1 = dv(lambda e, stg=stg, ginv=ginv: e.tensor_scalar(
                stg, stg, ginv, MAGICR, ALU.mult, ALU.add), waits=(t_gi,))
            t_c1 = dv(lambda e, stg=stg: e.tensor_scalar(
                stg, stg, MAGICR, NEG1R, ALU.subtract, ALU.max),
                waits=(t_d1,))
            t_c2 = dv(lambda e, stg=stg, c=c: e.tensor_scalar(
                wtnk[c % 2], stg, 1.0, None, ALU.min),
                waits=(t_c1, free.get(("wtnk", c % 2))))
            free[("stage", c % 4)] = t_c2
            t_g16c = dma_gd("g16s", g16_d[0:1, c * 128:(c + 1) * 128],
                            G16[:, c:c + 1], waits=(t_g,))
            stP2[c] = (t_c2, t_g16c)

        t_g16_last = [None]

        def III_P3(c):
            t_c2, t_g16c = stP2.pop(c)
            t_g16_last[0] = t_g16c
            last = None
            for h in range(2):
                t_tr = pe(lambda e, h=h, c=c: [
                    e.transpose(PTR[:, h, k * 128:(k + 1) * 128],
                                wtnk[c % 2][:, (h * 8 + k) * 128:(h * 8 + k + 1) * 128],
                                IDENT[:])
                    for k in range(8)][-1],
                    waits=(t_c2, free.get(("ptr", h))))
                t_ev = dv(lambda e, h=h, c=c: e.tensor_copy(
                    WB[:, h * 8:(h + 1) * 8, c * 128:(c + 1) * 128],
                    PTR[:, h, :].rearrange("p (a b) -> p a b", a=8)),
                    waits=(t_tr,) + (BAR2 if c <= 1 else ()))
                free[("ptr", h)] = t_ev
                last = t_ev
            free[("wtnk", c % 2)] = last
            return t_g16c

        III_P1(0)
        III_P1(1)

        for i in range(NT):
            fm = SCR[:, III_XNFM[i % 2], :].rearrange("p (c t) -> p c t", c=KC)
            t_xn = dma_sy("ds%d" % (2 + i % 2), SCR[:, III_XNFM[i % 2], :], xnsp(i),
                          waits=(free.get(("IIIxn", i % 2)),) + (BAR2 if i < 2 else ()))
            t_u_last = None
            for j in range(4):
                bank_ap = BANKS_A[j]
                pb = PHYS_A[j]
                t_mm = proj_mm(bank_ap, fm, WA, j, 2, korder2,
                               waits=(t_xn, free.get(("bank", pb)),
                                      (tok_wu_last if i == 0 and j == 0 else None)))
                t_u = ac(lambda e, j=j, bank_ap=bank_ap, i=i: e.activation(
                    utm_bufs[i % 2][:, j * 512:(j + 1) * 512], bank_ap,
                    AF.Silu), waits=(t_mm, free.get(("utm", i % 2))))
                free[("bank", pb)] = t_u
                t_u_last = t_u
            t_sp_u = dma_ac("sp%d" % (2 + i % 2), usp(i), utm_bufs[i % 2],
                            waits=(t_u_last,))
            free[("utm", i % 2)] = t_sp_u
            free[("IIIxn", i % 2)] = t_sp_u
            if i in (6, 7):
                qn = 2 + (i - 6)
                outx_toks.append(dma_gd(
                    "oxg", out_d[qn * 512:(qn + 1) * 512, :],
                    x_d[qn * 512:(qn + 1) * 512, :]))
            if i == 9:
                t_cci = dma_gd("ccd", cci_d, kvsb, waits=(t_kvev,))
                P.emit("gpsimd", lambda e: e.collective_compute(
                    "AllReduce", ALU.add,
                    replica_groups=[[0, 1, 2, 3], [4, 5, 6, 7]],
                    ins=[cci_d.opt()], outs=[cco_d.opt()]),
                    waits=(t_cci,), inc="cc")
            if i == 10:
                t_kvbf = dma_gd("ccd2", KVBF[:, :], cco_d, waits=(("cc", 1),))

            if 1 <= i <= 8:
                III_P3(2 * (i - 1))
                III_P3(2 * i - 1)
            if i <= 7:
                III_P2(2 * i)
                III_P2(2 * i + 1)
            if i <= 6:
                III_P1(2 * i + 2)
                III_P1(2 * i + 3)
            if i <= 5:
                III_wf(2 * i + 4)
                III_wf(2 * i + 5)
            if i == 8:
                t_g16d = t_g16_last[0]

        BAR3 = barrier()

        pe_fence3 = ("pe", P.counts.get("pe", 0))
        gamma_rep = scr_f32(4)
        t_grep = dma_sy("rep", gamma_rep,
                        bass.AP(g16_d.tensor, 0, [[0, 128], [1, E]]),
                        waits=(t_g16d, ("g16s", P.counts["g16s"])) + BAR3)

        IVA_XQTM, IVA_XQFM = (0, 1), (2, 3)
        xn_fs = [scr_f32(0), scr_f32(2)]
        b2_fs = [ex_f32(), ex_f32()]
        fworks = [scr_f32(6), scr_f32(8)]
        mmbank4 = [PMM[:, 0, :], PMM[:, 1, :], PKV[:, 0, :], PKV[:, 1, :]]

        stA2 = {}
        stB2 = {}
        stC2 = {}

        def IVa_A(i):
            xb = X2[:, i % 2, :]
            xn_f = xn_fs[i % 2]
            t_x = dma_sy("ds%d" % (i % 2), xb, x_d[i * 128:(i + 1) * 128, :],
                         waits=(free.get(("Ax", i % 2)),) + (BAR3 if i < 2 else ()))
            rstd, nb, t_nb, t_ss = ln_chain(xb, psum_dummy(0), t_x,
                                            free.get(("sqd", 0)))
            free[("sqd", 0)] = t_ss
            t_xnA = dv(lambda e, xb=xb, rstd=rstd, nb=nb, xn_f=xn_f:
                       e.tensor_scalar(xn_f, xb, rstd, nb, ALU.mult, ALU.add),
                       waits=(t_nb, free.get(("xnf", i % 2)))
                       + (BAR3 if i < 2 else ()))
            stA2[i] = t_xnA

        def IVa_B(i):
            t_xnA = stA2.pop(i)
            xb = X2[:, i % 2, :]
            xn_f = xn_fs[i % 2]
            b2_f = b2_fs[i % 2]
            ssn = stat(); lnr = stat(); irms = stat()
            amax = stat(); xsd = stat(); invd = stat()
            t_ssn = ac(lambda e, ssn=ssn, xn_f=xn_f: e.activation(
                psum_dummy(1), xn_f, AF.Square, accum_out=ssn),
                waits=(t_xnA, free.get(("sqd", 0))))
            free[("sqd", 0)] = t_ssn
            t_lnr = ac(lambda e, ssn=ssn, lnr=lnr: e.activation(
                lnr, ssn, AF.Ln, bias=1e-8, scale=INV_E), waits=(t_ssn,))
            t_irms = ac(lambda e, lnr=lnr, irms=irms: e.activation(
                irms, lnr, AF.Exp, scale=-0.5), waits=(t_lnr,))
            t_am = dv(lambda e, amax=amax, xn_f=xn_f: e.reduce_max(
                amax, xn_f, axis=AX.X, apply_absolute_value=True),
                waits=(t_xnA,))
            t_xs = dv(lambda e, amax=amax, irms=irms, i=i: e.tensor_scalar(
                XS16[:, i:i + 1], amax, irms, CONSTS[:, 6:7],
                ALU.mult, ALU.mult), waits=(t_am, t_irms))
            t_xr = ac(lambda e, irms=irms, xn_f=xn_f, b2_f=b2_f: e.activation(
                b2_f, xn_f, AF.Copy, scale=irms),
                waits=(t_irms, free.get(("b2f", 0)) or pe_fence3)
                + (BAR3 if i < 2 else ()))
            t_xsd = dv(lambda e, xsd=xsd, i=i: e.tensor_scalar(
                xsd, XS16[:, i:i + 1], 1e-8, None, ALU.add), waits=(t_xs,))
            t_invd = dv(lambda e, xsd=xsd, invd=invd: e.reciprocal(invd, xsd),
                        waits=(t_xsd,))
            t_zt = ac(lambda e, invd=invd, xb=xb, b2_f=b2_f: e.activation(
                xb, b2_f, AF.Copy, bias=MAGIC, scale=invd),
                waits=(t_invd, t_xr))
            free[("xnf", i % 2)] = t_xr
            t_c1 = dv(lambda e, xb=xb, b2_f=b2_f: e.tensor_scalar(
                b2_f, xb, MAGIC, -127.0, ALU.subtract, ALU.max),
                waits=(t_zt,))
            free[("Ax", i % 2)] = t_c1
            t_xq = dv(lambda e, b2_f=b2_f, i=i: e.tensor_scalar(
                wa_bf(IVA_XQTM[i % 2]), b2_f, 127.0, None, ALU.min),
                waits=(t_c1, free.get(("xqtm", i % 2)) or pe_fence3))
            free[("b2f", 0)] = t_xq
            stB2[i] = (t_xq, t_xs)

        def IVa_C1(i):
            t_xq, t_xs = stB2.pop(i)
            fmslot = IVA_XQFM[i % 2]
            t_xqfm = transpose_tile(wa_bf(IVA_XQTM[i % 2]), wa_fm(fmslot), t_xq,
                                    free.get(("xqfm", i % 2)) or pe_fence3, free,
                                    evac="ac")
            free[("xqtm", i % 2)] = t_xqfm
            t_mms = []
            for j in range(4):

                def ffn_fn(e, j=j, fmslot=fmslot):
                    last = None
                    for c in range(KC):
                        last = e.matmul(mmbank4[j], wa_fm(fmslot)[:, c, :],
                                        WB[:, c, j * 512:(j + 1) * 512],
                                        start=(c == 0), stop=(c == KC - 1))
                    return last
                t_mm = pe(ffn_fn, waits=(t_xqfm, free.get(("bank", PHYS_A[j]))))
                t_mms.append(t_mm)
            free[("xqfm", i % 2)] = t_mms[-1]
            stC2[i] = (t_mms, t_xs)

        def IVa_C2(i):
            t_mms, t_xs = stC2.pop(i)
            fw = fworks[i % 2]
            ev_last = None
            for j in range(4):
                t_f1 = dv(lambda e, j=j, i=i, fw=fw: e.scalar_tensor_tensor(
                    out=fw[:, j * 512:(j + 1) * 512], in0=mmbank4[j],
                    scalar=XS16[:, i:i + 1],
                    in1=gamma_rep[:, j * 512:(j + 1) * 512],
                    op0=ALU.mult, op1=ALU.mult),
                    waits=(t_mms[j], t_xs, free.get(("fw", i % 2)),
                           (t_grep if i == 0 else None))
                    + (BAR3 if i == 0 else ()))
                free[("bank", PHYS_A[j])] = t_f1
                ev_last = t_f1
            t_out = dma_gd("oa%d" % (i % 2), out_d[i * 128:(i + 1) * 128, :],
                           fw,
                           waits=(ev_last, ("ox", P.counts["ox"]),
                                  ("oxg", P.counts["oxg"])), accum=True)
            free[("fw", i % 2)] = t_out
            out_written.append(t_out)
            if i < 12:
                load_weight_chunk1("wta", WA, wo_d, 4 + i,
                                   waits=((pe_fence3,) if i == 0 else ()))

        out_written = []
        for k in range(NT + 3):
            if k >= 3:
                IVa_C2(k - 3)
            if 2 <= k <= NT + 1:
                IVa_C1(k - 2)
            if 1 <= k <= NT:
                IVa_B(k - 1)
            if k < NT:
                IVa_A(k)

        BAR4 = barrier()

        tok_wo_lo = load_weight_chunks("wta", WA, wo_d, 0, 4, waits=BAR4)
        pe_fence4 = ("pe", P.counts.get("pe", 0))
        gd_fence4 = ("gd", P.counts.get("gd", 0))
        lng_rep = scr_f32(6)
        lnb_rep = scr_f32(8)
        t_lgrep = dma_sy("rep", lng_rep,
                         bass.AP(lng_d.tensor, 0, [[0, 128], [1, E]]),
                         waits=BAR4)
        t_lbrep = dma_sy("rep", lnb_rep,
                         bass.AP(lnb_d.tensor, 0, [[0, 128], [1, E]]),
                         waits=BAR4)

        B_QIN, B_QFM, B_UIN, B_S, B_SFM = (0, 1), (2, 3), (4, 5), (6, 7), (8, 9)
        out_tms = [scr_f32(0), scr_f32(2)]
        t1_fs = [scr_f32(4), wb_f32(10)]
        attn_fs = [wb_f32(12), wb_f32(14)]
        psum_dummy_b = SQD[:, :]

        stF = {}
        stG = {}
        stH = {}

        def IVb_F(i):
            t_qi = dma_sy("ds%d" % (2 + i % 2), wb_bf(B_QIN[i % 2]), qsp(i),
                          waits=(free.get(("Bq", i % 2)),) + (BAR4 if i < 2 else ()))
            spall4 = ((("sp2", P.counts["sp2"]), ("sp3", P.counts["sp3"]))
                      if i < 2 else ())
            t_ui = dma_sy("ds%d" % (4 + i % 2), wb_bf(B_UIN[i % 2]), usp(i),
                          waits=(free.get(("Bu", i % 2)),) + spall4
                          + (BAR4 if i < 2 else ()))
            fmslot = B_QFM[i % 2]
            t_qfm = transpose_tile(wb_bf(B_QIN[i % 2]), wb_fm(fmslot), t_qi,
                                   free.get(("Bqfm", i % 2)) or pe_fence4, free,
                                   evac="ac")
            free[("Bq", i % 2)] = t_qfm

            out_tm = out_tms[i % 2]

            def oe_fn(e, fmslot=fmslot, half=0):
                last = None
                qfm = wb_fm(fmslot)
                for h in range(half * 8, half * 8 + 8):
                    last = e.matmul(
                        PKV[:, (h // 4) % 2, (h % 4) * 128:(h % 4 + 1) * 128],
                        qfm[:, h, :], KVBF[:, h * 128:(h + 1) * 128],
                        start=True, stop=True)
                return last
            t_oe1 = pe(lambda e, fmslot=fmslot: oe_fn(e, fmslot, 0),
                       waits=(t_qfm, free.get(("pkv", 0)),
                              (t_kvbf if i == 0 else None))
                       + (BAR4 if i == 0 else ()))
            t_oev1 = ac(lambda e, out_tm=out_tm: [
                e.activation(out_tm[:, b * 512:(b + 1) * 512], PKV[:, b, :],
                             AF.Copy)
                for b in range(2)][-1],
                waits=(t_oe1, free.get(("outtm", i % 2)))
                + (BAR4 if i < 2 else ()))
            t_oe2 = pe(lambda e, fmslot=fmslot: oe_fn(e, fmslot, 1),
                       waits=(t_oev1,))
            free[("Bqfm", i % 2)] = t_oe2
            t_oev = ac(lambda e, out_tm=out_tm: [
                e.activation(out_tm[:, 1024 + b * 512:1024 + (b + 1) * 512],
                             PKV[:, b, :], AF.Copy)
                for b in range(2)][-1],
                waits=(t_oe2,))
            free[("pkv", 0)] = t_oev
            stF[i] = (t_oev, t_ui)

        def IVb_G(i):
            t_oev, t_ui = stF.pop(i)
            out_tm = out_tms[i % 2]
            rstd, nb, t_nb, t_ss = ln_chain(out_tm, psum_dummy_b, t_oev,
                                            free.get(("sqd", 0)))
            free[("sqd", 0)] = t_ss
            t1_f = t1_fs[i % 2]
            t_t1 = dv(lambda e, rstd=rstd, nb=nb, out_tm=out_tm, t1_f=t1_f:
                      e.tensor_scalar(t1_f, out_tm, rstd, nb, ALU.mult, ALU.add),
                      waits=(t_nb, free.get(("t1f", i % 2)))
                      + (BAR4 if i < 2 else ()) + ((pe_fence4,) if i < 2 else ()))
            t_p1 = dv(lambda e, out_tm=out_tm, t1_f=t1_f: e.tensor_mul(
                out_tm, t1_f, lng_rep),
                waits=(t_t1, (t_lbrep if i == 0 else None)))
            free[("outtm", i % 2)] = t_p1
            t_p2 = dv(lambda e, out_tm=out_tm, t1_f=t1_f: e.tensor_add(
                t1_f, out_tm, lnb_rep),
                waits=(t_p1, (t_lbrep if i == 0 else None)))
            pass
            sslot = B_S[i % 2]
            t_s = dv(lambda e, sslot=sslot, t1_f=t1_f, i=i: e.tensor_mul(
                wb_bf(sslot), t1_f, wb_bf(B_UIN[i % 2])),
                waits=(t_p2, t_ui, free.get(("Bs", i % 2)) or pe_fence4))
            free[("Bu", i % 2)] = t_s
            free[("t1f", i % 2)] = t_s
            stG[i] = (t_s, sslot)

        def IVb_H1(i):
            t_s, sslot = stG.pop(i)
            sfm = B_SFM[i % 2]
            t_sfm = transpose_tile(wb_bf(sslot), wb_fm(sfm), t_s,
                                   free.get(("Bsfm", i % 2)) or pe_fence4, free,
                                   evac="ac")
            free[("Bs", i % 2)] = t_sfm
            attn_f = attn_fs[i % 2]
            WO_BANKS = [PMM[:, 0, :], PMM[:, 1, :], PKV[:, 2, :], PKV[:, 3, :]]
            WO_PHYS = (0, 1, 6, 7)
            t_mms = []
            for j in range(4):
                t_mm = proj_mm(WO_BANKS[j], wb_fm(sfm), WA, j, 3, range(KC),
                               waits=(t_sfm, free.get(("bank", WO_PHYS[j])),
                                      (tok_wo_lo if i == 0 and j == 0 else None)))
                t_mms.append(t_mm)
            free[("Bsfm", i % 2)] = t_mms[-1]
            stH[i] = t_mms

        def IVb_H2(i):
            t_mms = stH.pop(i)
            attn_f = attn_fs[i % 2]
            WO_BANKS = [PMM[:, 0, :], PMM[:, 1, :], PKV[:, 2, :], PKV[:, 3, :]]
            WO_PHYS = (0, 1, 6, 7)
            ev_last = None
            for j in range(4):
                t_ev = ac(lambda e, j=j, attn_f=attn_f: e.activation(
                    attn_f[:, j * 512:(j + 1) * 512], WO_BANKS[j], AF.Copy),
                    waits=(t_mms[j], free.get(("attnf", i % 2)))
                    + ((pe_fence4,) if i < 2 and j == 0 else ()))
                free[("bank", WO_PHYS[j])] = t_ev
                ev_last = t_ev
            t_acc = dma_gd("ob%d" % (i % 2), out_d[i * 128:(i + 1) * 128, :],
                           attn_f,
                           waits=(ev_last, out_written[i]), accum=True)
            free[("attnf", i % 2)] = t_acc

        for k in range(NT + 3):
            if k >= 3:
                IVb_H2(k - 3)
            if 2 <= k <= NT + 1:
                IVb_H1(k - 2)
            if 1 <= k <= NT:
                IVb_G(k - 1)
            if k < NT:
                IVb_F(k)

        finals = [(s, P.counts.get(s, 0)) for s in ("oa0", "oa1", "ob0", "ob1")]

        with nc.Block() as block:
            def mk(eng_name):
                def body(eng):
                    satisfied = {}
                    for fn, waits, inc, amt in P.steps[eng_name]:
                        for (s, v) in waits:
                            if satisfied.get(s, 0) >= v:
                                continue
                            eng.wait_ge(sems[s], v)
                            satisfied[s] = v
                        inst = fn(eng)
                        if inc is not None:
                            inst.then_inc(sems[inc], amt)
                    if eng_name == "gpsimd":
                        for s, v in finals:
                            eng.wait_ge(sems[s], v)
                return body

            block.sync(mk("sync"))
            block.gpsimd(mk("gpsimd"))
            block.tensor(mk("tensor"))
            block.vector(mk("vector"))
            block.scalar(mk("scalar"))

    return nc


_NC_CACHE = None


def kernel(L0_attn_x, L0_attn_Wqk, L0_attn_bqk, L0_attn_Wv, L0_attn_bv,
           L0_attn_Wu, L0_attn_bu, L0_attn_Wo, L0_attn_bo,
           L0_attn_ln_g, L0_attn_ln_b, L0_ffn_weight, L0_ffn_bias):
    global _NC_CACHE
    x = np.ascontiguousarray(np.asarray(L0_attn_x, dtype=np.float32))
    B, N, E_ = x.shape
    xf = x.reshape(B * N, E_)

    def arr(a):
        return np.ascontiguousarray(np.asarray(a, dtype=np.float32))

    def row(a):
        return arr(a).reshape(1, E_)

    consts = np.zeros((1, 16), np.float32)
    for i, v in enumerate(REG_CONSTS):
        consts[0, i] = v

    common = {
        "wqk": arr(L0_attn_Wqk), "wv": arr(L0_attn_Wv), "wu": arr(L0_attn_Wu),
        "wo": arr(L0_attn_Wo), "wf": arr(L0_ffn_weight),
        "bqk": row(L0_attn_bqk), "bv": row(L0_attn_bv), "bu": row(L0_attn_bu),
        "bo": row(L0_attn_bo), "fbias": row(L0_ffn_bias),
        "lng": row(L0_attn_ln_g), "lnb": row(L0_attn_ln_b),
        "cdb": consts,
        "sel": SEL_NP, "negm": NEGM_NP, "ident": IDENT_NP,
    }
    in_maps = []
    for c in range(8):
        m = dict(common)
        m["x"] = np.ascontiguousarray(xf[c * TOK:(c + 1) * TOK, :])
        in_maps.append(m)

    if _NC_CACHE is None:
        _NC_CACHE = build_nc()
    res = run_bass_kernel_spmd(_NC_CACHE, in_maps, core_ids=list(range(8)))
    out = np.concatenate([res.results[c]["out"] for c in range(8)], axis=0)
    return out.reshape(B, N, E_).astype(np.float32)


if __name__ == "__main__":
    nc = build_nc()
    print("built ok; instructions:",
          sum(1 for _ in nc.all_instructions()))
